# version 19 (final)
# baseline (speedup 1.0000x reference)
import os
import numpy as np
import ml_dtypes
import concourse.bass as bass
import concourse.mybir as mybir
from concourse.bass_utils import run_bass_kernel_spmd

F32 = mybir.dt.float32
BF16 = mybir.dt.bfloat16
FP8 = mybir.dt.float8e4
AF = mybir.ActivationFunctionType
ALU = mybir.AluOpType
AX = mybir.AxisListType.X

BF = ml_dtypes.bfloat16
MAGIC = float(3 * 2.0 ** 22)
CMASK = 64.0
NCORE = 8
D = 1024
H = 16
INNER = 4096
TILE = 128
STR = 256
GRP = 8


def _mask_consts():
    qm = np.zeros((9, H * STR), np.float32)
    km = np.zeros((9, H * STR), np.float32)
    t_of_col = np.arange(H * STR) % GRP
    for j in range(GRP):
        sel = t_of_col == j
        qm[j, sel] = CMASK
        km[j, sel] = CMASK
    qm[8, :] = CMASK
    km[8, :] = -CMASK
    return qm.astype(BF), km.astype(BF)


def build_program(n_tok, max_events=None, dbg=False):
    assert n_tok % STR == 0
    n_str = n_tok // STR
    race = bool(int(os.environ.get("KRACE", "0")))
    nc = bass.Bass("TRN2", target_bir_lowering=False, debug=False,
                   detect_race_conditions=race, dynamic_dma_scratch_size=2048)

    x_d = nc.dram_tensor("x", [n_tok, D], F32, kind="ExternalInput").ap()
    y_d = nc.dram_tensor("y", [n_tok, D], F32, kind="ExternalOutput").ap()
    wq_d = nc.dram_tensor("wq", [128, 64 * 128], F32, kind="ExternalInput").ap()
    wdn_d = nc.dram_tensor("wdn", [128, 16 * 128], F32, kind="ExternalInput").ap()
    wk_d = nc.dram_tensor("wk", [128, 16 * 128], F32, kind="ExternalInput").ap()
    wv_d = nc.dram_tensor("wv", [128, 16 * 128], F32, kind="ExternalInput").ap()
    wo_d = nc.dram_tensor("wo", [128, 64 * 128], F32, kind="ExternalInput").ap()
    wg_d = nc.dram_tensor("wg", [128, 8 * INNER], F32, kind="ExternalInput").ap()
    wu_d = nc.dram_tensor("wu", [128, 8 * INNER], F32, kind="ExternalInput").ap()
    wd_d = nc.dram_tensor("wd", [128, 32 * 1024], F32, kind="ExternalInput").ap()
    bias_d = nc.dram_tensor("bias", [128, 26], F32, kind="ExternalInput").ap()
    bout_d = nc.dram_tensor("bout", [1, D], F32, kind="ExternalInput").ap()

    qm_np, km_np = _mask_consts()
    qm_dd = nc.inline_tensor(qm_np, name="qmask_c").ap()
    km_dd = nc.inline_tensor(km_np, name="kmask_c").ap()
    ident_dd = nc.inline_tensor(np.eye(128, dtype=BF), name="ident_c").ap()
    onescol_dd = nc.inline_tensor(np.ones((128, 1), dtype=BF), name="onescol_c").ap()
    onesrow_dd = nc.inline_tensor(np.ones((1, 128), dtype=BF), name="onesrow_c").ap()

    A = nc.alloc_sbuf_tensor
    wq = A("wq_s", [128, 64 * 128], BF16)
    wdn = A("wdn_s", [128, 16 * 128], BF16)
    wk = A("wk_s", [128, 16 * 128], BF16)
    wv = A("wv_s", [128, 16 * 128], BF16)
    wo = A("wo_s", [128, 64 * 128], BF16)
    wg = A("wg_s", [128, 8 * INNER], FP8)
    wu = A("wu_s", [128, 8 * INNER], FP8)
    wd = A("wd_s", [128, 32 * 1024], FP8)
    ident = A("ident", [128, 128], BF16)
    onescol = A("onescol", [128, 1], BF16)
    onesrow = A("onesrow", [1, 128], BF16)
    biases = A("biases", [128, 26], F32)
    boutb = A("boutb", [1, D], BF16)
    wscal = A("wscal", [128, 4], F32)
    big = A("big", [128, INNER], F32)
    xbig = A("xbig", [128, 2 * D], F32)
    outb = A("outb", [128, 2 * D], F32)
    xn_tm = A("xn_tm", [128, D], BF16)
    xn_fm = A("xn_fm", [128, 8 * STR], BF16)
    lat_fm = A("lat_fm", [128, 2 * STR], BF16)
    qtmp = A("qtmp", [128, 2 * STR], BF16)
    q_blk = A("q_blk", [128, H * STR], BF16)
    k_blk = A("k_blk", [128, H * STR], BF16)
    v_bT = A("v_bT", [64, H * STR], BF16)
    v_bs = A("v_bs", [128, 2 * 4 * 65], BF16)
    probs = A("probs", [128, 512], BF16)
    av_sb = A("av_sb", [128, 4 * 64], BF16)
    av_fm = A("av_fm", [128, 8 * 128], BF16)
    tst = A("tst", [128, 1024], BF16)
    g2st = A("g2st", [128, 512], F32)
    xq2f = A("xq2f", [128, 1024], BF16)
    xq_fmbuf = A("xq_fmbuf", [128, 2 * D], BF16)
    stat = A("stat", [128, 52], F32)
    cst = A("cst", [128, 2], F32)
    onesf = A("onesf", [128, 1], F32)
    onesrf = xbig[0:1, 1024:1152]

    pA = nc.alloc_psum_tensor("pA", [128, 512], F32)
    pB = nc.alloc_psum_tensor("pB", [128, 512], F32)
    pS1 = nc.alloc_psum_tensor("pS1", [128, 512], F32)
    pS2 = nc.alloc_psum_tensor("pS2", [128, 512], F32)
    pAV = nc.alloc_psum_tensor("pAV", [128, 260], F32)
    pT1 = nc.alloc_psum_tensor("pT1", [128, 1024], BF16)
    pT2 = nc.alloc_psum_tensor("pT2", [128, 1024], BF16)
    pC = nc.alloc_psum_tensor("pC", [128, 512], F32)

    events = []
    cnt = {}

    def emit(eng, fn, waits=(), incs=()):
        for s_, a_ in incs:
            cnt[s_] = cnt.get(s_, 0) + a_
        events.append((eng, fn, tuple(waits), tuple(incs)))

    def now(s_):
        return cnt.get(s_, 0)

    def dma(dst, src, waits=(), sem="l0"):
        emit("sync", lambda e, d=dst, s=src: e.dma_start(out=d, in_=s),
             waits, [(sem, 16)])

    def T(fn, waits=(), inc=True):
        emit("tensor", fn, waits, [("t", 1)] if inc else ())

    def V(fn, waits=()):
        emit("vector", fn, waits, [("v", 1)])

    def Sc(fn, waits=()):
        emit("scalar", fn, waits, [("s", 1)])

    def G(fn, waits=()):
        emit("gpsimd", fn, waits, [("g", 1)])

    def VD():
        V(lambda e: e.tensor_copy(out=wscal[:, 3:4], in_=onesf[:, :]))

    dma(ident[:, :], ident_dd, sem="ldc")
    dma(onescol[:, :], onescol_dd, sem="ldc")
    dma(onesrow[:, :], onesrow_dd, sem="ldc")
    dma(biases[:, :], bias_d, sem="ldc")
    dma(xbig[0:1, 0:D], bout_d, sem="ldc")
    dma(q_blk[64:73, :], qm_dd, sem="ldc")
    dma(k_blk[64:73, :], km_dd, sem="ldc")
    ld0 = now("ldc")
    V(lambda e: e.memset(v_bs[:, :], 0.0))
    V(lambda e: e.memset(cst[:, 0:1], MAGIC))
    V(lambda e: e.memset(onesf[:, :], 1.0))
    V(lambda e: e.memset(onesrf, 1.0))
    V(lambda e: e.memset(cst[:, 1:2], 0.125))
    for g in range(8):
        V(lambda e, g=g: e.tensor_copy(out=v_bs[:, g * 65 + 64:g * 65 + 65],
                                       in_=onescol[:, :]), [("ldc", ld0)])
    V(lambda e: e.tensor_copy(out=boutb[:, :], in_=xbig[0:1, 0:D]), [("ldc", ld0)])
    boutb_v = now("v")

    bigq_free = [None, None, None, None]
    rrq = [0]

    def qdma(src_ap):
        r = rrq[0] % 4
        rrq[0] += 1
        dma(big[:, r * 1024:r * 1024 + src_ap.shape[1]], src_ap,
            [bigq_free[r]] if bigq_free[r] is not None else [],
            sem="l%d" % r)
        return r, ("l%d" % r, now("l%d" % r))

    def load_cast(dst, src_d, ncols):
        for off in range(0, ncols, 1024):
            w_ = min(1024, ncols - off)
            r, wt = qdma(src_d[:, off:off + w_])
            V(lambda e, o=off, w=w_, r=r, dd=dst: e.tensor_copy(
                out=dd[:, o:o + w], in_=big[:, r * 1024:r * 1024 + w]), [wt])
            bigq_free[r] = ("v", now("v"))

    cast_quads = []
    for dst_, srcd_, nc_ in ((wq, wq_d, 64 * 128), (wdn, wdn_d, 16 * 128),
                             (wk, wk_d, 16 * 128), (wv, wv_d, 16 * 128),
                             (wo, wo_d, 64 * 128)):
        for off_ in range(0, nc_, 1024):
            cast_quads.append((dst_, srcd_, off_, min(1024, nc_ - off_)))

    def emit_cast(k):
        for _ in range(k):
            if not cast_quads:
                return
            dst_, srcd_, off_, w_ = cast_quads.pop(0)
            r, wt = qdma(srcd_[:, off_:off_ + w_])
            V(lambda e, o=off_, w=w_, r=r, dd=dst_: e.tensor_copy(
                out=dd[:, o:o + w], in_=big[:, r * 1024:r * 1024 + w]), [wt])
            bigq_free[r] = ("v", now("v"))

    def tern_p1_quad(src_d, q, c0):
        emit_cast(1)
        r, wt = qdma(src_d[:, q * 1024:(q + 1) * 1024])
        Sc(lambda e, r=r, c=c0 + r: e.activation(
            out=xn_tm[:, :], in_=big[:, r * 1024:(r + 1) * 1024],
            func=AF.Abs, bias=0.0, scale=1.0,
            accum_out=stat[:, c:c + 1]), [wt])
        bigq_free[r] = ("s", now("s"))
        V(lambda e, r=r, c0=c0: e.tensor_add(
            out=stat[:, c0 + 4 + r:c0 + 5 + r],
            in0=stat[:, c0 + 4 + r:c0 + 5 + r],
            in1=stat[:, c0 + r:c0 + 1 + r]), [("s", now("s"))])

    def tern_pass1(src_d, ncols, c0):
        nq = ncols // 1024
        for r4 in range(4):
            V(lambda e, c=c0 + 4 + r4: e.memset(stat[:, c:c + 1], 0.0))
        for q in range(nq):
            tern_p1_quad(src_d, q, c0)

    def tern_total(ncols, c0, scol):
        VD()
        V(lambda e, c0=c0: e.tensor_add(out=stat[:, c0 + 8:c0 + 9],
                                        in0=stat[:, c0 + 4:c0 + 5],
                                        in1=stat[:, c0 + 5:c0 + 6]))
        V(lambda e, c0=c0: e.tensor_add(out=stat[:, c0 + 9:c0 + 10],
                                        in0=stat[:, c0 + 6:c0 + 7],
                                        in1=stat[:, c0 + 7:c0 + 8]))
        VD()
        V(lambda e, c0=c0: e.tensor_add(out=stat[:, c0 + 8:c0 + 9],
                                        in0=stat[:, c0 + 8:c0 + 9],
                                        in1=stat[:, c0 + 9:c0 + 10]))
        T(lambda e, c0=c0: e.matmul(out=pAV[0:1, 0:1],
                                    lhsT=stat[:, c0 + 8:c0 + 9],
                                    rhs=onesf[:, :], start=True, stop=True),
          [("v", now("v"))])
        V(lambda e, c0=c0: e.tensor_copy(out=stat[0:1, c0 + 10:c0 + 11],
                                         in_=pAV[0:1, 0:1]), [("t", now("t"))])
        T(lambda e, c0=c0: e.matmul(out=pAV[:, 1:2], lhsT=onesrf,
                                    rhs=stat[0:1, c0 + 10:c0 + 11],
                                    start=True, stop=True), [("v", now("v"))])
        V(lambda e, c0=c0, n=128 * ncols: e.tensor_scalar(
            out=stat[:, c0 + 11:c0 + 12], in0=pAV[:, 1:2], scalar1=1.0 / n,
            scalar2=1e-5, op0=ALU.mult, op1=ALU.max), [("t", now("t"))])
        VD()
        V(lambda e, c0=c0, sc=scol: e.reciprocal(out=wscal[:, sc:sc + 1],
                                                 in_=stat[:, c0 + 11:c0 + 12]))

    def tern_p2_quad(dst, src_d, q, scol):
        r, wt = qdma(src_d[:, q * 1024:(q + 1) * 1024])
        V(lambda e, sc=scol, r=r: e.tensor_scalar(
            out=xq2f[:, :],
            in0=big[:, r * 1024:(r + 1) * 1024],
            scalar1=wscal[:, sc:sc + 1], scalar2=0.5,
            op0=ALU.mult, op1=ALU.is_ge), [wt])
        V(lambda e, sc=scol, r=r: e.tensor_scalar(
            out=av_fm[:, :],
            in0=big[:, r * 1024:(r + 1) * 1024],
            scalar1=wscal[:, sc:sc + 1], scalar2=-0.5,
            op0=ALU.mult, op1=ALU.is_le))
        bigq_free[r] = ("v", now("v"))
        VD()
        V(lambda e, dd=dst, q=q: e.tensor_sub(
            out=dd[:, q * 1024:(q + 1) * 1024],
            in0=xq2f[:, :], in1=av_fm[:, :]))

    tern_pass1(wg_d, 8 * INNER, 0)
    emit_cast(64)
    tern_total(8 * INNER, 0, 0)
    VD()
    for r4 in range(4):
        V(lambda e, c=12 + 4 + r4: e.memset(stat[:, c:c + 1], 0.0))
    for q in range(32):
        tern_p2_quad(wg, wg_d, q, 0)
        tern_p1_quad(wu_d, q, 12)
    tern_total(8 * INNER, 12, 1)
    VD()
    for r4 in range(4):
        V(lambda e, c=24 + 4 + r4: e.memset(stat[:, c:c + 1], 0.0))
    for q in range(32):
        tern_p2_quad(wu, wu_d, q, 1)
        tern_p1_quad(wd_d, q, 24)
    tern_total(32 * 1024, 24, 2)
    VD()
    for q in range(32):
        tern_p2_quad(wd, wd_d, q, 2)

    eps = 1e-5
    x_ld = {}
    xlast_v = {}
    xlast_s = {}
    sq_s = {}
    xnT_t = {}
    xqT_t = {}
    xq_evac_s = {}
    ffn_gu_t = {}
    wout_rd_t = [0]
    pt1_free_v = [now("v")]
    pt2_free = [("v", now("v"))]
    pA_free = [("v", now("v"))]
    pB_free = [("v", now("v"))]
    pC_free = [("v", now("v"))]
    ps1_free_v = [now("v")]
    ps1_owner_s = [0]
    ps2_free_v = [now("v")]
    pav_free_v = [now("v")]
    pav_free_g = [now("g")]
    xnfm_rd_t = [0]
    latfm_rd_t = [0]
    last_round_s = [0]
    store_cnt = {}
    store_last = {0: None, 1: None}
    st_of = lambda i: "st%d" % (i % 2)
    ldx_of = lambda i: "ldx%d" % (i % 2)

    qtmp_last = {0: None, 1: None}

    def load_x(i):
        if i - 2 >= 0:
            w = [("v", xlast_v[i - 2]), ("s", xlast_s[i - 2])]
        else:
            w = [("v", boutb_v)] if i == 0 else []
        dma(xbig[:, (i % 2) * D:(i % 2 + 1) * D],
            x_d[i * TILE:(i + 1) * TILE, :], w, sem=ldx_of(i))
        x_ld[i] = now(ldx_of(i))

    xnb_v_d = {}

    def front_stats_gen(i):
        til = i % 2
        so = 32 * til
        xv = xbig[:, til * D:(til + 1) * D]
        st_ = lambda c: stat[:, so + c:so + c + 1]
        fw = [(ldx_of(i), x_ld[i])]
        V(lambda e: e.reduce_sum(out=st_(0), in_=xv, axis=AX),
          fw + ([("ldc", ld0)] if i == 0 else []))
        yield
        Sc(lambda e: e.activation(out=xn_tm[:, :], in_=xv, func=AF.Square,
                                  bias=0.0, scale=1.0, accum_out=st_(2)), fw)
        sq_s[i] = now("s")
        yield
        V(lambda e: e.reduce_max(out=st_(5), in_=xv, axis=AX))
        yield
        V(lambda e: e.tensor_reduce(out=st_(6), in_=xv, axis=AX, op=ALU.min))
        yield
        V(lambda e: e.tensor_scalar_mul(out=st_(1), in0=st_(0), scalar1=1.0 / D))
        yield
        V(lambda e: e.tensor_scalar_mul(out=st_(3), in0=st_(2), scalar1=1.0 / D),
          [("s", sq_s[i])])
        yield
        V(lambda e: e.tensor_mul(out=st_(16), in0=st_(1), in1=st_(1)))
        yield
        V(lambda e: e.tensor_sub(out=st_(7), in0=st_(5), in1=st_(1)))
        yield
        V(lambda e: e.tensor_sub(out=st_(3), in0=st_(3), in1=st_(16)))
        yield
        V(lambda e: e.tensor_sub(out=st_(18), in0=st_(1), in1=st_(6)))
        yield
        V(lambda e: e.tensor_scalar_add(out=st_(3), in0=st_(3), scalar1=eps))
        yield
        V(lambda e: e.tensor_max(out=st_(7), in0=st_(7), in1=st_(18)))
        yield
        V(lambda e: e.tensor_scalar(out=st_(16), in0=st_(3), scalar1=0.5,
                                    scalar2=0.5, op0=ALU.mult, op1=ALU.add))
        yield
        VD()
        V(lambda e: e.reciprocal(out=st_(4), in_=st_(16)))
        yield
        for _ in range(4):
            VD()
            V(lambda e: e.tensor_mul(out=st_(17), in0=st_(4), in1=st_(4)))
            yield
            VD()
            V(lambda e: e.tensor_mul(out=st_(17), in0=st_(17), in1=st_(3)))
            yield
            VD()
            V(lambda e: e.tensor_scalar(out=st_(17), in0=st_(17), scalar1=-0.5,
                                        scalar2=1.5, op0=ALU.mult, op1=ALU.add))
            yield
            VD()
            V(lambda e: e.tensor_mul(out=st_(4), in0=st_(4), in1=st_(17)))
            yield
        VD()
        V(lambda e, xv=xv: e.tensor_scalar(
            out=xn_tm[:, :], in0=xv, scalar1=st_(1), scalar2=st_(4),
            op0=ALU.subtract, op1=ALU.mult),
          [("t", xqT_t.get(i - 1, 0)), ("s", sq_s[i])])
        xnb_v_d[i] = now("v")
        yield
        V(lambda e: e.tensor_mul(out=st_(8), in0=st_(7), in1=st_(4)))
        yield
        VD()
        V(lambda e: e.tensor_scalar_max(out=st_(8), in0=st_(8), scalar1=1e-5))
        yield
        VD()
        V(lambda e: e.reciprocal(out=st_(19), in_=st_(8)))
        yield
        VD()
        V(lambda e: e.tensor_scalar_mul(out=st_(9), in0=st_(19), scalar1=127.0))
        yield
        VD()
        V(lambda e: e.tensor_mul(out=st_(16), in0=st_(9), in1=wscal[:, 0:1]))
        yield
        V(lambda e: e.tensor_mul(out=st_(10), in0=st_(9), in1=st_(4)))
        yield
        V(lambda e: e.reciprocal(out=st_(11), in_=st_(16)))
        yield
        V(lambda e: e.tensor_mul(out=st_(17), in0=st_(9), in1=wscal[:, 1:2]))
        yield
        V(lambda e: e.tensor_scalar_mul(out=st_(11), in0=st_(11), scalar1=0.5))
        yield
        V(lambda e: e.reciprocal(out=st_(12), in_=st_(17)))
        yield

    pend = [None]

    def drain(n):
        g = pend[0]
        if g is None:
            return
        for _ in range(n):
            if next(g, StopIteration) is StopIteration:
                pend[0] = None
                return

    def drain_all():
        drain(1000)

    def rest_a(i):
        til = i % 2
        for c in range(8):
            T(lambda e, c=c: e.transpose(out=pT1[:, c * 128:(c + 1) * 128],
                                         in_=xn_tm[:, c * 128:(c + 1) * 128],
                                         identity=ident[:, :]),
              ([("v", xnb_v_d[i]), ("v", pt1_free_v[0])]
               + ([("ldc", ld0)] if i == 0 else [])) if c == 0 else (),
              inc=(c == 7))
        xnT_t[i] = now("t")
        V(lambda e, til=til: e.tensor_copy(
            out=xn_fm[:, :].rearrange("p (c t) -> p c t", c=8)[:, :, til * 128:(til + 1) * 128],
            in_=pT1[:, :].rearrange("p (c t) -> p c t", c=8)),
          [("t", xnT_t[i]), ("t", xnfm_rd_t[0])])
        pt1_free_v[0] = now("v")

    def rest_b(i):
        til = i % 2
        so = 32 * til
        xv = xbig[:, til * D:(til + 1) * D]
        st_ = lambda c: stat[:, so + c:so + c + 1]
        yb = outb[:, (1 - til) * D:(2 - til) * D]
        V(lambda e, xv=xv, yb=yb: e.tensor_scalar(
            out=yb, in0=xv, scalar1=st_(1), scalar2=st_(10),
            op0=ALU.subtract, op1=ALU.mult),
          [store_last[1 - til]] if store_last[1 - til] is not None else [])
        vy0 = now("v")
        Sc(lambda e, yb=yb: e.activation(out=yb, in_=yb,
                                         func=AF.Identity, bias=cst[:, 0:1],
                                         scale=1.0),
           [("v", vy0)])
        V(lambda e, yb=yb: e.tensor_scalar_add(out=xn_tm[:, :], in0=yb,
                                               scalar1=-MAGIC),
          [("s", now("s")), ("t", xnT_t[i])])
        q1_v = now("v")
        for c in range(8):
            T(lambda e, c=c: e.transpose(out=pT2[:, c * 128:(c + 1) * 128],
                                         in_=xn_tm[:, c * 128:(c + 1) * 128],
                                         identity=ident[:, :]),
              ([("v", q1_v), pt2_free[0]]) if c == 0 else (),
              inc=(c == 7))
        xqT_t[i] = now("t")
        Sc(lambda e, til=til: e.copy(
            out=xq_fmbuf[:, til * D:(til + 1) * D], in_=pT2[:, :]),
           [("t", xqT_t[i]), ("t", ffn_gu_t.get(i - 2, 0))])
        xq_evac_s[i] = now("s")
        pt2_free[0] = ("s", xq_evac_s[i])

    def stretch_proj(mid=None):
        xnfm_v = pt1_free_v[0]
        for cp in range(8):
            pj = pA if cp % 2 == 0 else pB
            pf = pA_free if cp % 2 == 0 else pB_free
            tw = [("v", xnfm_v), pf[0]]
            for kc in range(8):
                T(lambda e, cp=cp, kc=kc, pj=pj: e.matmul(
                    out=pj[:, 0:STR],
                    lhsT=wq[:, (kc * 8 + cp) * 128:(kc * 8 + cp + 1) * 128],
                    rhs=xn_fm[:, kc * STR:(kc + 1) * STR],
                    start=(kc == 0), stop=(kc == 7)), tw if kc == 0 else (),
                  inc=(kc == 7))
            par = cp % 2
            qo = par * STR
            ew = [("t", now("t"))]
            if qtmp_last[par] is not None:
                ew += list(qtmp_last[par])
            V(lambda e, cp=cp, pj=pj, qo=qo: e.tensor_scalar(
                out=qtmp[:, qo:qo + STR], in0=pj[:, 0:STR],
                scalar1=biases[:, cp:cp + 1],
                scalar2=None, op0=ALU.add), ew)
            vv = now("v")
            pf[0] = ("v", vv)
            G(lambda e, cp=cp, qo=qo: e.tensor_copy(
                out=q_blk[0:64, :].rearrange("p (grp c) -> p grp c", c=128)[:, :, (2 * cp) * 8:(2 * cp + 1) * 8],
                in_=qtmp[0:64, qo:qo + STR].rearrange("p (grp t) -> p grp t", t=8)),
              [("v", vv)])
            Sc(lambda e, cp=cp, qo=qo: e.copy(
                out=q_blk[0:64, :].rearrange("p (grp c) -> p grp c", c=128)[:, :, (2 * cp + 1) * 8:(2 * cp + 2) * 8],
                in_=qtmp[64:128, qo:qo + STR].rearrange("p (grp t) -> p grp t", t=8)),
               [("v", vv)])
            qtmp_last[par] = (("g", now("g")), ("s", now("s")))
            if cp == 1 and mid is not None:
                mid()
        for lc in range(2):
            pj = pA if lc % 2 == 0 else pB
            pf = pA_free if lc % 2 == 0 else pB_free
            tw = [pf[0]]
            for kc in range(8):
                T(lambda e, lc=lc, kc=kc, pj=pj: e.matmul(
                    out=pj[:, 0:STR],
                    lhsT=wdn[:, (kc * 2 + lc) * 128:(kc * 2 + lc + 1) * 128],
                    rhs=xn_fm[:, kc * STR:(kc + 1) * STR],
                    start=(kc == 0), stop=(kc == 7)), tw if kc == 0 else (),
                  inc=(kc == 7))
            V(lambda e, lc=lc, pj=pj: e.tensor_scalar(
                out=lat_fm[:, lc * STR:(lc + 1) * STR], in0=pj[:, 0:STR],
                scalar1=biases[:, 8 + lc:9 + lc], scalar2=None, op0=ALU.add),
              [("t", now("t")), ("t", latfm_rd_t[0])] if lc == 0 else
              [("t", now("t"))])
            pf[0] = ("v", now("v"))
        xnfm_rd_t[0] = now("t")
        lat_v = now("v")
        npair = [0]
        for wmat, bc, isk in ((wk, 10, True), (wv, 18, False)):
            for cp in range(8):
                rot = npair[0] % 4
                npair[0] += 1
                pj = (pA, pB, pS1, pS2)[rot]
                tw = [("v", lat_v)]
                if rot == 0:
                    pf = pA_free
                    tw += [pf[0]]
                elif rot == 1:
                    pf = pB_free
                    tw += [pf[0]]
                elif rot == 2:
                    pf = None
                    tw += [("v", ps1_free_v[0]), ("s", ps1_owner_s[0])]
                else:
                    pf = None
                    tw += [("v", ps2_free_v[0])]
                for kc in range(2):
                    T(lambda e, cp=cp, kc=kc, pj=pj, wmat=wmat: e.matmul(
                        out=pj[:, 0:STR],
                        lhsT=wmat[:, (kc * 8 + cp) * 128:(kc * 8 + cp + 1) * 128],
                        rhs=lat_fm[:, kc * STR:(kc + 1) * STR],
                        start=(kc == 0), stop=(kc == 1)), tw if kc == 0 else (),
                      inc=(kc == 1))
                par = cp % 2
                qo = par * STR
                ew = [("t", now("t"))]
                if qtmp_last[par] is not None:
                    ew += list(qtmp_last[par])
                V(lambda e, cp=cp, pj=pj, bc=bc, qo=qo: e.tensor_scalar(
                    out=qtmp[:, qo:qo + STR], in0=pj[:, 0:STR],
                    scalar1=biases[:, bc + cp:bc + cp + 1],
                    scalar2=None, op0=ALU.add), ew)
                vv = now("v")
                if pf is not None:
                    pf[0] = ("v", vv)
                elif rot == 2:
                    ps1_free_v[0] = vv
                else:
                    ps2_free_v[0] = vv
                dst = k_blk if isk else v_bT
                VD()
                V(lambda e, cp=cp, qo=qo, dst=dst: e.tensor_copy(
                    out=dst[0:64, :].rearrange("p (grp c) -> p grp c", c=128)[:, :, (2 * cp) * 8:(2 * cp + 1) * 8],
                    in_=qtmp[0:64, qo:qo + STR].rearrange("p (grp t) -> p grp t", t=8)))
                Sc(lambda e, cp=cp, qo=qo, dst=dst: e.copy(
                    out=dst[0:64, :].rearrange("p (grp c) -> p grp c", c=128)[:, :, (2 * cp + 1) * 8:(2 * cp + 2) * 8],
                    in_=qtmp[64:128, qo:qo + STR].rearrange("p (grp t) -> p grp t", t=8)), [("v", vv)])
                qtmp_last[par] = (("v", now("v")), ("s", now("s")))
        latfm_rd_t[0] = now("t")
        return now("g"), now("s"), now("v")

    vT_t = {}
    vbs_v = {}
    sc_t = {}
    exp_s = {}
    av_t = {}
    norm_v = {}
    norm_g = {}
    avT_t = {}
    avfm_v = {}

    def head_vt(si, b, gmarsh, smarsh, vmarsh):
        g0 = b * 4
        tw = [("v", pt1_free_v[0])]
        if b == 0:
            tw += [("g", gmarsh), ("s", smarsh), ("v", vmarsh)]
        for gg in range(4):
            g = g0 + gg
            T(lambda e, g=g, gg=gg: e.transpose(
                out=pT1[:, gg * 64:(gg + 1) * 64],
                in_=v_bT[0:64, g * 128:(g + 1) * 128],
                identity=ident[0:64, 0:64]), tw if gg == 0 else (),
              inc=(gg == 3))
        vT_t[b] = now("t")

    def head_vbs(b, extra_t):
        o = (b % 2) * 260
        V(lambda e, o=o: e.tensor_copy(
            out=v_bs[:, o:o + 260].rearrange("p (g c) -> p g c", g=4)[:, :, 0:64],
            in_=pT1[:, 0:256].rearrange("p (g c) -> p g c", g=4)),
          [("t", max(vT_t[b], extra_t))])
        vbs_v[b] = now("v")
        pt1_free_v[0] = vbs_v[b]

    def head_scores(si, b):
        g0 = b * 4
        sw = [("v", vbs_v[b]), ("v", ps1_free_v[0])]
        if b >= 1:
            sw += [("s", exp_s[b - 1][1])]
        if si == 0 and b == 0:
            sw += [("ldc", ld0)]
        for gg in range(4):
            g = g0 + gg
            T(lambda e, g=g, gg=gg: e.matmul(
                out=pS1[:, gg * 128:(gg + 1) * 128],
                lhsT=k_blk[0:73, g * 128:(g + 1) * 128],
                rhs=q_blk[0:73, g * 128:(g + 1) * 128],
                start=True, stop=True), sw if gg == 0 else (),
              inc=(gg == 3))
        sc_t[b] = now("t")
        exp_h = []
        for h in range(2):
            Sc(lambda e, h=h: e.activation(
                out=probs[:, h * 256:(h + 1) * 256],
                in_=pS1[:, h * 256:(h + 1) * 256], func=AF.Exp,
                bias=0.0, scale=cst[:, 1:2]),
               [("t", sc_t[b])] if h == 0 else ())
            exp_h.append(now("s"))
        exp_s[b] = (exp_h[0], exp_h[1])
        ps1_owner_s[0] = exp_h[1]

    def attn_head(si, b, gmarsh, smarsh, vmarsh):
        head_vt(si, b, gmarsh, smarsh, vmarsh)
        head_vbs(b, 0)
        head_scores(si, b)

    def attn_step(si, b, gmarsh, smarsh, vmarsh, with_head):
        g0 = b * 4
        if with_head:
            head_vt(si, b + 1, gmarsh, smarsh, vmarsh)
        for gg in range(4):
            aw = ()
            if gg == 0:
                aw = [("s", exp_s[b][0]), ("v", pav_free_v[0])]
            elif gg == 2:
                aw = [("s", exp_s[b][1])]
            T(lambda e, gg=gg, o=(b % 2) * 260: e.matmul(
                out=pAV[:, gg * 65:(gg + 1) * 65],
                lhsT=probs[:, gg * 128:(gg + 1) * 128],
                rhs=v_bs[:, o + gg * 65:o + (gg + 1) * 65],
                start=True, stop=True), aw, inc=(gg == 3))
        av_t[b] = now("t")
        if with_head:
            head_vbs(b + 1, av_t.get(b - 1, 0))
            head_scores(si, b + 1)
        V(lambda e: e.reciprocal(out=stat[:, 28:32], in_=pAV[:, 64:260:65]),
          [("t", av_t[b])])
        VD()
        nv = []
        for gg in range(4):
            V(lambda e, gg=gg: e.tensor_scalar(
                out=av_sb[:, gg * 64:(gg + 1) * 64],
                in0=pAV[:, gg * 65:gg * 65 + 64],
                scalar1=stat[:, 28 + gg:29 + gg], scalar2=None, op0=ALU.mult))
            nv.append(now("v"))
        norm_v[b] = now("v")
        pav_free_v[0] = norm_v[b]
        for gg in range(4):
            pw = [("v", nv[gg])]
            if gg == 0:
                pw += [avfm_v[b - 1] if b >= 1 else pt2_free[0]]
            T(lambda e, gg=gg: e.transpose(
                out=pT2[0:64, gg * 128:(gg + 1) * 128],
                in_=av_sb[:, gg * 64:(gg + 1) * 64],
                identity=ident[:, :]),
              pw, inc=(gg == 3))
        avT_t[b] = now("t")
        ew = [("t", avT_t[b])]
        if b % 4 == 0:
            ew += [("t", wout_rd_t[0])]
        gl0 = g0 % 16
        for p in range(2):
            V(lambda e, p=p, gl0=gl0: e.tensor_copy(
                out=av_fm[p * 64:(p + 1) * 64, :]
                    .rearrange("q (c gl2 t) -> q gl2 c t", c=8, gl2=16)[:, gl0:gl0 + 4, :, :],
                in_=pT2[0:64, 0:512]
                    .rearrange("q (gg c2 pp t) -> q gg c2 pp t", gg=4, c2=8, pp=2, t=8)[:, :, :, p, :]),
              ew if p == 0 else ())
        avfm_v[b] = ("v", now("v"))
        pt2_free[0] = avfm_v[b]
        drain(3)

    vs2_d = {}

    def ffn_a(i, avfm_ready):
        til = i % 2
        so = 32 * til
        st_ = lambda c: stat[:, so + c:so + c + 1]
        xv = xbig[:, til * D:(til + 1) * D]
        ss = (1 - til) * D
        stg = outb[:, ss:ss + D]
        for half in range(2):
            pj = pA if half == 0 else pB
            pf = pA_free if half == 0 else pB_free
            tw = [avfm_ready, pf[0]]
            T(lambda e, half=half, pj=pj: e.matmul(
                out=pj[:, :], lhsT=onesrow[:, :],
                rhs=boutb[:, half * 512:(half + 1) * 512],
                start=True, stop=False), tw, inc=False)
            for c in range(8):
                T(lambda e, c=c, half=half, pj=pj: e.matmul(
                    out=pj[:, :], lhsT=av_fm[:, c * 128:(c + 1) * 128],
                    rhs=wo[:, (c * 8 + half * 4) * 128:(c * 8 + half * 4 + 4) * 128],
                    start=False, stop=(c == 7)), inc=(c == 7))
        wout_t = now("t")
        wout_rd_t[0] = wout_t
        stw = [("t", wout_t)]
        if i - 1 >= 0:
            stw += [(st_of(i - 1), store_cnt[i - 1])]
        V(lambda e, stg=stg: e.scalar_tensor_tensor(
            out=stg[:, 0:512], in0=pA[:, :], scalar=1.0, in1=xv[:, 0:512],
            op0=ALU.mult, op1=ALU.add), stw)
        stage1_v = now("v")
        pA_free[0] = ("v", stage1_v)
        V(lambda e, stg=stg: e.scalar_tensor_tensor(
            out=stg[:, 512:1024], in0=pB[:, :], scalar=1.0, in1=xv[:, 512:1024],
            op0=ALU.mult, op1=ALU.add))
        stage2_v = now("v")
        pB_free[0] = ("v", stage2_v)
        xlast_v[i] = stage2_v
        xlast_s[i] = sq_s[i]
        g2_v = {}
        stt2_v = {}
        gate_t = {}
        up_t = {}
        tanh_s = {}
        for j in range(8):
            gw = [("v", g2_v[j - 1] if j >= 1 else stage1_v)]
            if j == 0:
                gw += [("s", xq_evac_s[i])]
            for c in range(8):
                T(lambda e, c=c, j=j, til=til: e.matmul(
                    out=pA[:, :], lhsT=xq_fmbuf[:, til * D + c * 128:til * D + (c + 1) * 128],
                    rhs=wg[:, c * INNER + j * 512:c * INNER + (j + 1) * 512],
                    start=(c == 0), stop=(c == 7)), gw if c == 0 else (),
                  inc=(c == 7))
            gate_t[j] = now("t")
            pu = pB if j % 2 == 0 else pC
            puf = pB_free if j % 2 == 0 else pC_free
            uw = [puf[0]]
            for c in range(8):
                T(lambda e, c=c, j=j, til=til, pu=pu: e.matmul(
                    out=pu[:, :], lhsT=xq_fmbuf[:, til * D + c * 128:til * D + (c + 1) * 128],
                    rhs=wu[:, c * INNER + j * 512:c * INNER + (j + 1) * 512],
                    start=(c == 0), stop=(c == 7)), uw if c == 0 else (),
                  inc=(c == 7))
            up_t[j] = now("t")
            V(lambda e: e.tensor_scalar(out=g2st[:, :], in0=pA[:, :],
                                        scalar1=st_(11), scalar2=None,
                                        op0=ALU.mult), [("t", gate_t[j])])
            g2_v[j] = now("v")
            Sc(lambda e: e.activation(out=tst[:, 0:512], in_=g2st[:, :],
                                      func=AF.Tanh,
                                      bias=0.0, scale=1.0), [("v", g2_v[j])])
            tanh_s[j] = now("s")
            if j >= 1:
                V(lambda e, j=j: e.tensor_reduce(
                    out=stat[:, 33:34], in_=big[:, (j - 1) * 512:j * 512],
                    axis=AX, op=ALU.max, apply_absolute_value=True))
            else:
                VD()
            V(lambda e: e.scalar_tensor_tensor(
                out=g2st[:, :], in0=tst[:, 0:512], scalar=1.0, in1=g2st[:, :],
                op0=ALU.add, op1=ALU.mult), [("s", tanh_s[j])])
            if j >= 2:
                V(lambda e: e.tensor_max(out=stat[:, 34:35], in0=stat[:, 34:35],
                                         in1=stat[:, 33:34]))
            elif j == 1:
                V(lambda e: e.tensor_copy(out=stat[:, 34:35], in_=stat[:, 33:34]))
            else:
                VD()
            V(lambda e, j=j, pu=pu: e.scalar_tensor_tensor(
                out=big[:, j * 512:(j + 1) * 512], in0=pu[:, :], scalar=st_(12),
                in1=g2st[:, :], op0=ALU.mult, op1=ALU.mult),
              [("t", up_t[j]), ("s", last_round_s[0])])
            stt2_v[j] = now("v")
            puf[0] = ("v", stt2_v[j])
            drain(2)
        ffn_gu_t[i] = up_t[7]
        pA_free[0] = ("v", g2_v[7])
        VD()
        V(lambda e: e.tensor_reduce(
            out=stat[:, 33:34], in_=big[:, 7 * 512:8 * 512],
            axis=AX, op=ALU.max, apply_absolute_value=True))
        VD()
        V(lambda e: e.tensor_max(out=stat[:, 34:35], in0=stat[:, 34:35],
                                 in1=stat[:, 33:34]))
        VD()
        V(lambda e: e.tensor_scalar_max(out=stat[:, 35:36], in0=stat[:, 34:35],
                                        scalar1=1e-5))
        VD()
        V(lambda e: e.reciprocal(out=stat[:, 36:37], in_=stat[:, 35:36]))
        VD()
        V(lambda e: e.tensor_scalar_mul(out=st_(13), in0=stat[:, 36:37],
                                        scalar1=127.0))
        VD()
        V(lambda e: e.tensor_mul(out=stat[:, 37:38], in0=st_(13),
                                 in1=wscal[:, 2:3]))
        VD()
        V(lambda e: e.reciprocal(out=st_(14), in_=stat[:, 37:38]))
        vs2_d[i] = now("v")

    def ffn_b(i, mid=None):
        til = i % 2
        so = 32 * til
        st_ = lambda c: stat[:, so + c:so + c + 1]
        os_ = til * D
        ss = (1 - til) * D
        stg = outb[:, ss:ss + D]
        vs2 = vs2_d[i]
        sub_v = {}
        xq2f_v = {}
        trans_t = {}
        round_s = {}
        down_t = {}
        for j in range(9):
            if j < 8:
                par = j % 2
                Sc(lambda e, j=j: e.activation(
                    out=g2st[:, :], in_=big[:, j * 512:(j + 1) * 512],
                    func=AF.Identity, bias=cst[:, 0:1], scale=st_(13)),
                   [("v", vs2 if j == 0 else sub_v[j - 1])])
                round_s[j] = now("s")
                V(lambda e, par=par: e.tensor_scalar_add(
                    out=tst[:, par * 512:(par + 1) * 512], in0=g2st[:, :],
                    scalar1=-MAGIC),
                  [("s", round_s[j]), ("t", trans_t.get(j - 2, 0))])
                sub_v[j] = now("v")
                tw = [("v", sub_v[j])]
                if j >= 2:
                    tw += [("v", xq2f_v[j - 2])]
                else:
                    tw += [("v", pt1_free_v[0])]
                for c in range(4):
                    T(lambda e, c=c, par=par: e.transpose(
                        out=pT1[:, par * 512 + c * 128:par * 512 + (c + 1) * 128],
                        in_=tst[:, par * 512 + c * 128:par * 512 + (c + 1) * 128],
                        identity=ident[:, :]), tw if c == 0 else (),
                      inc=(c == 3))
                trans_t[j] = now("t")
                V(lambda e, par=par: e.tensor_copy(
                    out=xq2f[:, par * 512:(par + 1) * 512],
                    in_=pT1[:, par * 512:(par + 1) * 512]),
                  [("t", trans_t[j]), ("t", down_t.get(j - 2, 0))])
                xq2f_v[j] = now("v")
                pt1_free_v[0] = xq2f_v[j]
                if j >= 2:
                    drain(3)
            if j >= 1:
                jj = j - 1
                dw = [("v", xq2f_v[jj])]
                if jj == 0:
                    dw += [("s", ps1_owner_s[0]), ("v", ps2_free_v[0])]
                for c in range(4):
                    ic = jj * 4 + c
                    for half in range(2):
                        pj = pS1 if half == 0 else pS2
                        T(lambda e, c=c, jj=jj, ic=ic, half=half, pj=pj: e.matmul(
                            out=pj[:, :],
                            lhsT=xq2f[:, (jj % 2) * 512 + c * 128:(jj % 2) * 512 + (c + 1) * 128],
                            rhs=wd[:, ic * 1024 + half * 512:ic * 1024 + (half + 1) * 512],
                            start=(ic == 0), stop=(ic == 31)),
                          dw if (c == 0 and half == 0) else (),
                          inc=(c == 3 and half == 1))
                down_t[jj] = now("t")
                if jj == 0 and mid is not None:
                    mid()
        last_round_s[0] = round_s[7]
        tdown = now("t")
        V(lambda e, os_=os_, stg=stg: e.scalar_tensor_tensor(
            out=outb[:, os_:os_ + 512], in0=pS1[:, :], scalar=st_(14),
            in1=stg[:, 0:512], op0=ALU.mult, op1=ALU.add), [("t", tdown)])
        ps1_free_v[0] = now("v")
        V(lambda e, os_=os_, stg=stg: e.scalar_tensor_tensor(
            out=outb[:, os_ + 512:os_ + 1024], in0=pS2[:, :], scalar=st_(14),
            in1=stg[:, 512:1024], op0=ALU.mult, op1=ALU.add))
        ps2_free_v[0] = now("v")
        dma(y_d[i * TILE:(i + 1) * TILE, :], outb[:, os_:os_ + D],
            [("v", now("v"))], sem=st_of(i))
        store_cnt[i] = now(st_of(i))
        store_last[til] = (st_of(i), store_cnt[i])

    n_tiles = n_tok // TILE
    carry_rest_b = [None]
    for si in range(n_str):
        i0, i1 = 2 * si, 2 * si + 1
        if si == 0:
            load_x(i0)
            load_x(i1)
            pend[0] = front_stats_gen(i0)
            drain_all()
            rest_a(i0)
            rest_b(i0)
            pend[0] = front_stats_gen(i1)
            drain_all()
            rest_a(i1)
            rest_b(i1)
        gmarsh, smarsh, vmarsh = stretch_proj(mid=carry_rest_b[0])
        carry_rest_b[0] = None
        attn_head(si, 0, gmarsh, smarsh, vmarsh)
        for b in range(4):
            attn_step(si, b, gmarsh, smarsh, vmarsh, with_head=True)
        ffn_a(i0, avfm_v[3])
        if i0 + 2 < n_tiles:
            load_x(i0 + 2)
            pend[0] = front_stats_gen(i0 + 2)
        attn_step(si, 4, gmarsh, smarsh, vmarsh, with_head=True)
        attn_step(si, 5, gmarsh, smarsh, vmarsh, with_head=True)
        ffn_b(i0)
        attn_step(si, 6, gmarsh, smarsh, vmarsh, with_head=True)
        attn_step(si, 7, gmarsh, smarsh, vmarsh, with_head=False)
        ffn_a(i1, avfm_v[7])
        if i1 + 2 < n_tiles:
            load_x(i1 + 2)
            drain_all()
            rest_a(i0 + 2)

            def mid_cb(i2=i0 + 2, i3=i1 + 2):
                rest_b(i2)
                pend[0] = front_stats_gen(i3)
            ffn_b(i1, mid=mid_cb)
            drain_all()
            rest_a(i1 + 2)
            carry_rest_b[0] = (lambda i3=i1 + 2: rest_b(i3))
        else:
            ffn_b(i1)
    emit("sync", lambda e: e.nop(),
         [("st0", now("st0")), ("st1", now("st1"))], ())

    if max_events is not None:
        del events[max_events:]
    if dbg:
        dbg_list = [('xn_tm', xn_tm), ('stat', stat),
                    ('xn_fm', xn_fm), ('qtmp', qtmp), ('q_blk', q_blk),
                    ('k_blk', k_blk), ('v_bT', v_bT), ('lat_fm', lat_fm),
                    ('probs', probs), ('av_sb', av_sb), ('av_fm', av_fm),
                    ('big', big), ('tst', tst), ('g2st', g2st),
                    ('outb', outb), ('v_bs', v_bs), ('xq_fmbuf', xq_fmbuf),
                    ('xq2f', xq2f), ('wscal', wscal), ('xbig', xbig),
                    ('biases_s', biases)]
        import os as _os
        _sel = _os.environ.get('KDBG', '')
        if _sel:
            dbg_list = [(n_, t_) for n_, t_ in dbg_list if n_ in _sel.split(',')]
        pre = []
        _noinit = {'q_blk', 'k_blk', 'v_bs', 'big', 'xbig', 'biases_s'}
        for nm, tt_ in dbg_list:
            if nm in _noinit:
                continue
            pre.append(("gpsimd", (lambda s_: (lambda e: e.memset(s_[:, :], 0.0)))(tt_),
                        (), (("g", 0),)))
        events[0:0] = pre
        allw = [(k_, now(k_)) for k_ in
                ('t', 'v', 's', 'g', 'ldc', 'l0', 'l1', 'l2', 'l3',
                 'ldx0', 'ldx1', 'st0', 'st1')]
        _rows = {'q_blk': 73, 'k_blk': 73}
        for nm, tt_ in dbg_list:
            ap_ = tt_.ap()
            r_ = _rows.get(nm, ap_.shape[0])
            dd = nc.dram_tensor('dbg_' + nm, [r_, ap_.shape[1]], ap_.dtype,
                                kind='ExternalOutput').ap()
            emit('sync', lambda e, d=dd, s_=tt_, r_=r_: e.dma_start(out=d, in_=s_[0:r_, :]),
                 allw, [('st0', 16)])
        emit('sync', lambda e: e.nop(), [('st0', now('st0'))], ())
    sem_names = ["ldc", "l0", "l1", "l2", "l3", "ldx0", "ldx1",
                 "st0", "st1", "t", "v", "s", "g"]
    import contextlib
    with contextlib.ExitStack() as stack:
        sems = {n_: stack.enter_context(nc.semaphore(n_)) for n_ in sem_names}
        block = stack.enter_context(nc.Block())

        def runner(name):
            def run(eng):
                for e_name, fn, waits, incs in events:
                    if e_name != name:
                        continue
                    for sname, val in waits:
                        if val > 0:
                            eng.wait_ge(sems[sname], val)
                    ins = fn(eng)
                    for sname, amt in incs:
                        ins.then_inc(sems[sname], amt)
            return run

        block.sync(runner("sync"))
        block.tensor(runner("tensor"))
        block.vector(runner("vector"))
        block.scalar(runner("scalar"))
        block.gpsimd(runner("gpsimd"))
    return nc


def _prep_weights(inp):
    w = {}
    wq = np.asarray(inp["L0_attn_Wq"], np.float32)
    w["wq"] = np.ascontiguousarray(
        wq.reshape(8, 128, 8, 128).transpose(1, 0, 2, 3).reshape(128, 64 * 128))
    wdn = np.asarray(inp["L0_attn_Wdown"], np.float32)
    w["wdn"] = np.ascontiguousarray(
        wdn.reshape(8, 128, 2, 128).transpose(1, 0, 2, 3).reshape(128, 16 * 128))
    wup = np.asarray(inp["L0_attn_Wup"], np.float32).reshape(256, 16, 2, 64)
    wkm = np.ascontiguousarray(wup[:, :, 0].reshape(256, 1024))
    wvm = np.ascontiguousarray(wup[:, :, 1].reshape(256, 1024))
    w["wk"] = np.ascontiguousarray(
        wkm.reshape(2, 128, 8, 128).transpose(1, 0, 2, 3).reshape(128, 16 * 128))
    w["wv"] = np.ascontiguousarray(
        wvm.reshape(2, 128, 8, 128).transpose(1, 0, 2, 3).reshape(128, 16 * 128))
    wo = np.asarray(inp["L0_attn_Wout"], np.float32)
    w["wo"] = np.ascontiguousarray(
        wo.reshape(8, 128, 8, 128).transpose(1, 0, 2, 3).reshape(128, 64 * 128))
    wgT = np.ascontiguousarray(np.asarray(inp["L0_ffn_w_gate"], np.float32).T)
    wuT = np.ascontiguousarray(np.asarray(inp["L0_ffn_w_up"], np.float32).T)
    wdT = np.ascontiguousarray(np.asarray(inp["L0_ffn_w_down"], np.float32).T)
    w["wg"] = np.ascontiguousarray(
        wgT.reshape(8, 128, 4096).transpose(1, 0, 2).reshape(128, 8 * 4096))
    w["wu"] = np.ascontiguousarray(
        wuT.reshape(8, 128, 4096).transpose(1, 0, 2).reshape(128, 8 * 4096))
    w["wd"] = np.ascontiguousarray(
        wdT.reshape(32, 128, 1024).transpose(1, 0, 2).reshape(128, 32 * 1024))
    bias = np.zeros((128, 26), np.float32)
    bias[:, 0:8] = np.asarray(inp["L0_attn_bq"], np.float32).reshape(8, 128).T
    bias[:, 8:10] = np.asarray(inp["L0_attn_bdown"], np.float32).reshape(2, 128).T
    bup = np.asarray(inp["L0_attn_bup"], np.float32).reshape(16, 2, 64)
    bias[:, 10:18] = np.ascontiguousarray(bup[:, 0].reshape(-1)).reshape(8, 128).T
    bias[:, 18:26] = np.ascontiguousarray(bup[:, 1].reshape(-1)).reshape(8, 128).T
    w["bias"] = bias
    w["bout"] = np.asarray(inp["L0_attn_bout"], np.float32).reshape(1, D)
    return w


def kernel(**inputs):
    x = np.asarray(inputs["L0_attn_x"], np.float32)
    B, T_, D_ = x.shape
    n_tok = (B * T_) // NCORE
    w = _prep_weights(inputs)
    nc = build_program(n_tok)
    xs = x.reshape(NCORE, n_tok, D_)
    in_maps = [dict(w, x=np.ascontiguousarray(xs[c])) for c in range(NCORE)]
    res = run_bass_kernel_spmd(nc, in_maps, core_ids=list(range(NCORE)))
    y = np.stack([np.asarray(r["y"]) for r in res.results])
    return np.ascontiguousarray(y.reshape(B, T_, D_))


# version 20
# speedup vs baseline: 1.1959x; 1.1959x over previous
import os
import numpy as np
import ml_dtypes
import concourse.bass as bass
import concourse.mybir as mybir
from concourse.bass_utils import run_bass_kernel_spmd

F32 = mybir.dt.float32
BF16 = mybir.dt.bfloat16
FP8 = mybir.dt.float8e4
AF = mybir.ActivationFunctionType
ALU = mybir.AluOpType
AX = mybir.AxisListType.X

BF = ml_dtypes.bfloat16
MAGIC = float(3 * 2.0 ** 22)
CMASK = 64.0
NCORE = 8
D = 1024
H = 16
INNER = 4096
TILE = 128
STR = 256
GRP = 8


def _mask_consts():
    qm = np.zeros((9, H * STR), np.float32)
    km = np.zeros((9, H * STR), np.float32)
    t_of_col = np.arange(H * STR) % GRP
    for j in range(GRP):
        sel = t_of_col == j
        qm[j, sel] = CMASK
        km[j, sel] = CMASK
    qm[8, :] = CMASK
    km[8, :] = -CMASK
    return qm.astype(BF), km.astype(BF)


def build_program(n_tok, max_events=None, dbg=False):
    assert n_tok % STR == 0
    n_str = n_tok // STR
    race = bool(int(os.environ.get("KRACE", "0")))
    nc = bass.Bass("TRN2", target_bir_lowering=False, debug=False,
                   detect_race_conditions=race, dynamic_dma_scratch_size=2048)

    x_d = nc.dram_tensor("x", [n_tok, D], F32, kind="ExternalInput").ap()
    y_d = nc.dram_tensor("y", [n_tok, D], F32, kind="ExternalOutput").ap()
    wq_d = nc.dram_tensor("wq", [128, 64 * 128], F32, kind="ExternalInput").ap()
    wdn_d = nc.dram_tensor("wdn", [128, 16 * 128], F32, kind="ExternalInput").ap()
    wk_d = nc.dram_tensor("wk", [128, 16 * 128], F32, kind="ExternalInput").ap()
    wv_d = nc.dram_tensor("wv", [128, 16 * 128], F32, kind="ExternalInput").ap()
    wo_d = nc.dram_tensor("wo", [128, 64 * 128], F32, kind="ExternalInput").ap()
    wg_d = nc.dram_tensor("wg", [128, 8 * INNER], F32, kind="ExternalInput").ap()
    wu_d = nc.dram_tensor("wu", [128, 8 * INNER], F32, kind="ExternalInput").ap()
    wd_d = nc.dram_tensor("wd", [128, 32 * 1024], F32, kind="ExternalInput").ap()
    bias_d = nc.dram_tensor("bias", [128, 26], F32, kind="ExternalInput").ap()
    bout_d = nc.dram_tensor("bout", [1, D], F32, kind="ExternalInput").ap()

    qm_np, km_np = _mask_consts()
    qm_dd = nc.inline_tensor(qm_np, name="qmask_c").ap()
    km_dd = nc.inline_tensor(km_np, name="kmask_c").ap()
    ident_dd = nc.inline_tensor(np.eye(128, dtype=BF), name="ident_c").ap()
    onescol_dd = nc.inline_tensor(np.ones((128, 1), dtype=BF), name="onescol_c").ap()
    onesrow_dd = nc.inline_tensor(np.ones((1, 128), dtype=BF), name="onesrow_c").ap()

    A = nc.alloc_sbuf_tensor
    wq = A("wq_s", [128, 64 * 128], BF16)
    wdn = A("wdn_s", [128, 16 * 128], BF16)
    wk = A("wk_s", [128, 16 * 128], BF16)
    wv = A("wv_s", [128, 16 * 128], BF16)
    wo = A("wo_s", [128, 64 * 128], BF16)
    wg = A("wg_s", [128, 8 * INNER], FP8)
    wu = A("wu_s", [128, 8 * INNER], FP8)
    wd = A("wd_s", [128, 32 * 1024], FP8)
    ident = A("ident", [128, 128], BF16)
    onescol = A("onescol", [128, 1], BF16)
    onesrow = A("onesrow", [1, 128], BF16)
    biases = A("biases", [128, 26], F32)
    boutb = A("boutb", [1, D], BF16)
    wscal = A("wscal", [128, 4], F32)
    big = A("big", [128, INNER], F32)
    xbig = A("xbig", [128, 2 * D], F32)
    outb = A("outb", [128, 2 * D], F32)
    xn_tm = A("xn_tm", [128, D], BF16)
    xn_fm = A("xn_fm", [128, 8 * STR], BF16)
    lat_fm = A("lat_fm", [128, 2 * STR], BF16)
    qtmp = A("qtmp", [128, 2 * STR], BF16)
    q_blk = A("q_blk", [128, H * STR], BF16)
    k_blk = A("k_blk", [128, H * STR], BF16)
    v_bT = A("v_bT", [64, H * STR], BF16)
    v_bs = A("v_bs", [128, 2 * 4 * 65], BF16)
    probs = A("probs", [128, 512], BF16)
    av_sb = A("av_sb", [128, 4 * 64], BF16)
    av_fm = A("av_fm", [128, 8 * 128], BF16)
    tst = A("tst", [128, 1024], BF16)
    g2st = A("g2st", [128, 512], F32)
    xq2f = A("xq2f", [128, 1024], BF16)
    xq_fmbuf = A("xq_fmbuf", [128, 2 * D], BF16)
    stat = A("stat", [128, 52], F32)
    cst = A("cst", [128, 2], F32)
    onesf = A("onesf", [128, 1], F32)
    onesrf = xbig[0:1, 1024:1152]

    pA = nc.alloc_psum_tensor("pA", [128, 512], F32)
    pB = nc.alloc_psum_tensor("pB", [128, 512], F32)
    pS1 = nc.alloc_psum_tensor("pS1", [128, 512], F32)
    pS2 = nc.alloc_psum_tensor("pS2", [128, 512], F32)
    pAV = nc.alloc_psum_tensor("pAV", [128, 260], F32)
    pT1 = nc.alloc_psum_tensor("pT1", [128, 1024], BF16)
    pT2 = nc.alloc_psum_tensor("pT2", [128, 1024], BF16)
    pC = nc.alloc_psum_tensor("pC", [128, 512], F32)

    events = []
    cnt = {}

    def emit(eng, fn, waits=(), incs=()):
        for s_, a_ in incs:
            cnt[s_] = cnt.get(s_, 0) + a_
        events.append((eng, fn, tuple(waits), tuple(incs)))

    def now(s_):
        return cnt.get(s_, 0)

    def dma(dst, src, waits=(), sem="l0"):
        emit("sync", lambda e, d=dst, s=src: e.dma_start(out=d, in_=s),
             waits, [(sem, 16)])

    def T(fn, waits=(), inc=True):
        emit("tensor", fn, waits, [("t", 1)] if inc else ())

    def V(fn, waits=()):
        emit("vector", fn, waits, [("v", 1)])

    def Sc(fn, waits=()):
        emit("scalar", fn, waits, [("s", 1)])

    def G(fn, waits=()):
        emit("gpsimd", fn, waits, [("g", 1)])

    def VD():
        V(lambda e: e.tensor_copy(out=wscal[:, 3:4], in_=onesf[:, :]))

    dma(ident[:, :], ident_dd, sem="ldc")
    dma(onescol[:, :], onescol_dd, sem="ldc")
    dma(onesrow[:, :], onesrow_dd, sem="ldc")
    dma(biases[:, :], bias_d, sem="ldc")
    dma(xbig[0:1, 0:D], bout_d, sem="ldc")
    dma(q_blk[64:73, :], qm_dd, sem="ldc")
    dma(k_blk[64:73, :], km_dd, sem="ldc")
    ld0 = now("ldc")
    V(lambda e: e.memset(v_bs[:, :], 0.0))
    V(lambda e: e.memset(cst[:, 0:1], MAGIC))
    V(lambda e: e.memset(onesf[:, :], 1.0))
    V(lambda e: e.memset(onesrf, 1.0))
    V(lambda e: e.memset(cst[:, 1:2], 0.125))
    for g in range(8):
        V(lambda e, g=g: e.tensor_copy(out=v_bs[:, g * 65 + 64:g * 65 + 65],
                                       in_=onescol[:, :]), [("ldc", ld0)])
    V(lambda e: e.tensor_copy(out=boutb[:, :], in_=xbig[0:1, 0:D]), [("ldc", ld0)])
    boutb_v = now("v")

    bigq_free = [None, None, None, None]
    rrq = [0]

    def qdma(src_ap):
        r = rrq[0] % 4
        rrq[0] += 1
        dma(big[:, r * 1024:r * 1024 + src_ap.shape[1]], src_ap,
            [bigq_free[r]] if bigq_free[r] is not None else [],
            sem="l%d" % r)
        return r, ("l%d" % r, now("l%d" % r))

    def load_cast(dst, src_d, ncols):
        for off in range(0, ncols, 1024):
            w_ = min(1024, ncols - off)
            r, wt = qdma(src_d[:, off:off + w_])
            V(lambda e, o=off, w=w_, r=r, dd=dst: e.tensor_copy(
                out=dd[:, o:o + w], in_=big[:, r * 1024:r * 1024 + w]), [wt])
            bigq_free[r] = ("v", now("v"))

    cast_quads = []
    for dst_, srcd_, nc_ in ((wq, wq_d, 64 * 128), (wdn, wdn_d, 16 * 128),
                             (wk, wk_d, 16 * 128), (wv, wv_d, 16 * 128),
                             (wo, wo_d, 64 * 128)):
        for off_ in range(0, nc_, 1024):
            cast_quads.append((dst_, srcd_, off_, min(1024, nc_ - off_)))

    def emit_cast(k):
        for _ in range(k):
            if not cast_quads:
                return
            dst_, srcd_, off_, w_ = cast_quads.pop(0)
            r, wt = qdma(srcd_[:, off_:off_ + w_])
            V(lambda e, o=off_, w=w_, r=r, dd=dst_: e.tensor_copy(
                out=dd[:, o:o + w], in_=big[:, r * 1024:r * 1024 + w]), [wt])
            bigq_free[r] = ("v", now("v"))

    def tern_p1_quad(src_d, q, c0):
        emit_cast(1)
        r, wt = qdma(src_d[:, q * 1024:(q + 1) * 1024])
        Sc(lambda e, r=r, c=c0 + r: e.activation(
            out=xn_tm[:, :], in_=big[:, r * 1024:(r + 1) * 1024],
            func=AF.Abs, bias=0.0, scale=1.0,
            accum_out=stat[:, c:c + 1]), [wt])
        bigq_free[r] = ("s", now("s"))
        V(lambda e, r=r, c0=c0: e.tensor_add(
            out=stat[:, c0 + 4 + r:c0 + 5 + r],
            in0=stat[:, c0 + 4 + r:c0 + 5 + r],
            in1=stat[:, c0 + r:c0 + 1 + r]), [("s", now("s"))])

    def tern_pass1(src_d, ncols, c0):
        nq = ncols // 1024
        for r4 in range(4):
            V(lambda e, c=c0 + 4 + r4: e.memset(stat[:, c:c + 1], 0.0))
        for q in range(nq):
            tern_p1_quad(src_d, q, c0)

    def tern_total(ncols, c0, scol):
        VD()
        V(lambda e, c0=c0: e.tensor_add(out=stat[:, c0 + 8:c0 + 9],
                                        in0=stat[:, c0 + 4:c0 + 5],
                                        in1=stat[:, c0 + 5:c0 + 6]))
        V(lambda e, c0=c0: e.tensor_add(out=stat[:, c0 + 9:c0 + 10],
                                        in0=stat[:, c0 + 6:c0 + 7],
                                        in1=stat[:, c0 + 7:c0 + 8]))
        VD()
        V(lambda e, c0=c0: e.tensor_add(out=stat[:, c0 + 8:c0 + 9],
                                        in0=stat[:, c0 + 8:c0 + 9],
                                        in1=stat[:, c0 + 9:c0 + 10]))
        T(lambda e, c0=c0: e.matmul(out=pAV[0:1, 0:1],
                                    lhsT=stat[:, c0 + 8:c0 + 9],
                                    rhs=onesf[:, :], start=True, stop=True),
          [("v", now("v"))])
        V(lambda e, c0=c0: e.tensor_copy(out=stat[0:1, c0 + 10:c0 + 11],
                                         in_=pAV[0:1, 0:1]), [("t", now("t"))])
        T(lambda e, c0=c0: e.matmul(out=pAV[:, 1:2], lhsT=onesrf,
                                    rhs=stat[0:1, c0 + 10:c0 + 11],
                                    start=True, stop=True), [("v", now("v"))])
        V(lambda e, c0=c0, n=128 * ncols: e.tensor_scalar(
            out=stat[:, c0 + 11:c0 + 12], in0=pAV[:, 1:2], scalar1=1.0 / n,
            scalar2=1e-5, op0=ALU.mult, op1=ALU.max), [("t", now("t"))])
        VD()
        V(lambda e, c0=c0, sc=scol: e.reciprocal(out=wscal[:, sc:sc + 1],
                                                 in_=stat[:, c0 + 11:c0 + 12]))

    def tern_p2_quad(dst, src_d, q, scol):
        r, wt = qdma(src_d[:, q * 1024:(q + 1) * 1024])
        V(lambda e, sc=scol, r=r: e.tensor_scalar(
            out=xq2f[:, :],
            in0=big[:, r * 1024:(r + 1) * 1024],
            scalar1=wscal[:, sc:sc + 1], scalar2=0.5,
            op0=ALU.mult, op1=ALU.is_ge), [wt])
        V(lambda e, sc=scol, r=r: e.tensor_scalar(
            out=av_fm[:, :],
            in0=big[:, r * 1024:(r + 1) * 1024],
            scalar1=wscal[:, sc:sc + 1], scalar2=-0.5,
            op0=ALU.mult, op1=ALU.is_le))
        bigq_free[r] = ("v", now("v"))
        VD()
        V(lambda e, dd=dst, q=q: e.tensor_sub(
            out=dd[:, q * 1024:(q + 1) * 1024],
            in0=xq2f[:, :], in1=av_fm[:, :]))

    tern_pass1(wg_d, 8 * INNER, 0)
    emit_cast(64)
    tern_total(8 * INNER, 0, 0)
    VD()
    for r4 in range(4):
        V(lambda e, c=12 + 4 + r4: e.memset(stat[:, c:c + 1], 0.0))
    for q in range(32):
        tern_p2_quad(wg, wg_d, q, 0)
        tern_p1_quad(wu_d, q, 12)
    tern_total(8 * INNER, 12, 1)
    VD()
    for r4 in range(4):
        V(lambda e, c=24 + 4 + r4: e.memset(stat[:, c:c + 1], 0.0))
    for q in range(32):
        tern_p2_quad(wu, wu_d, q, 1)
        tern_p1_quad(wd_d, q, 24)
    tern_total(32 * 1024, 24, 2)
    VD()
    for q in range(32):
        tern_p2_quad(wd, wd_d, q, 2)

    eps = 1e-5
    x_ld = {}
    xlast_v = {}
    xlast_s = {}
    sq_s = {}
    xnT_t = {}
    xqT_t = {}
    xq_evac_s = {}
    ffn_gu_t = {}
    wout_rd_t = [0]
    pt1_free_v = [now("v")]
    pt2_free = [("v", now("v"))]
    pA_free = [("v", now("v"))]
    pB_free = [("v", now("v"))]
    pC_free = [("v", now("v"))]
    ps1_free_v = [now("v")]
    ps1_owner_s = [0]
    ps2_free_v = [now("v")]
    pav_free_v = [now("v")]
    pav_free_g = [now("g")]
    xnfm_rd_t = [0]
    latfm_rd_t = [0]
    last_round_s = [0]
    store_cnt = {}
    store_last = {0: None, 1: None}
    st_of = lambda i: "st%d" % (i % 2)
    ldx_of = lambda i: "ldx%d" % (i % 2)

    qtmp_last = {0: None, 1: None}

    def load_x(i):
        if i - 2 >= 0:
            w = [("v", xlast_v[i - 2]), ("s", xlast_s[i - 2])]
        else:
            w = [("v", boutb_v)] if i == 0 else []
        dma(xbig[:, (i % 2) * D:(i % 2 + 1) * D],
            x_d[i * TILE:(i + 1) * TILE, :], w, sem=ldx_of(i))
        x_ld[i] = now(ldx_of(i))

    xnb_v_d = {}

    def front_stats_gen(i):
        til = i % 2
        so = 32 * til
        xv = xbig[:, til * D:(til + 1) * D]
        st_ = lambda c: stat[:, so + c:so + c + 1]
        fw = [(ldx_of(i), x_ld[i])]
        V(lambda e: e.reduce_sum(out=st_(0), in_=xv, axis=AX),
          fw + ([("ldc", ld0)] if i == 0 else []))
        yield
        Sc(lambda e: e.activation(out=xn_tm[:, :], in_=xv, func=AF.Square,
                                  bias=0.0, scale=1.0, accum_out=st_(2)), fw)
        sq_s[i] = now("s")
        yield
        V(lambda e: e.reduce_max(out=st_(5), in_=xv, axis=AX))
        yield
        V(lambda e: e.tensor_reduce(out=st_(6), in_=xv, axis=AX, op=ALU.min))
        yield
        V(lambda e: e.tensor_scalar_mul(out=st_(1), in0=st_(0), scalar1=1.0 / D))
        yield
        V(lambda e: e.tensor_scalar_mul(out=st_(3), in0=st_(2), scalar1=1.0 / D),
          [("s", sq_s[i])])
        yield
        V(lambda e: e.tensor_mul(out=st_(16), in0=st_(1), in1=st_(1)))
        yield
        V(lambda e: e.tensor_sub(out=st_(7), in0=st_(5), in1=st_(1)))
        yield
        V(lambda e: e.tensor_sub(out=st_(3), in0=st_(3), in1=st_(16)))
        yield
        V(lambda e: e.tensor_sub(out=st_(18), in0=st_(1), in1=st_(6)))
        yield
        V(lambda e: e.tensor_scalar_add(out=st_(3), in0=st_(3), scalar1=eps))
        yield
        V(lambda e: e.tensor_max(out=st_(7), in0=st_(7), in1=st_(18)))
        yield
        V(lambda e: e.tensor_scalar(out=st_(16), in0=st_(3), scalar1=0.5,
                                    scalar2=0.5, op0=ALU.mult, op1=ALU.add))
        yield
        VD()
        V(lambda e: e.reciprocal(out=st_(4), in_=st_(16)))
        yield
        for _ in range(4):
            VD()
            V(lambda e: e.tensor_mul(out=st_(17), in0=st_(4), in1=st_(4)))
            yield
            VD()
            V(lambda e: e.tensor_mul(out=st_(17), in0=st_(17), in1=st_(3)))
            yield
            VD()
            V(lambda e: e.tensor_scalar(out=st_(17), in0=st_(17), scalar1=-0.5,
                                        scalar2=1.5, op0=ALU.mult, op1=ALU.add))
            yield
            VD()
            V(lambda e: e.tensor_mul(out=st_(4), in0=st_(4), in1=st_(17)))
            yield
        VD()
        V(lambda e, xv=xv: e.tensor_scalar(
            out=xn_tm[:, :], in0=xv, scalar1=st_(1), scalar2=st_(4),
            op0=ALU.subtract, op1=ALU.mult),
          [("t", xqT_t.get(i - 1, 0)), ("s", sq_s[i])])
        xnb_v_d[i] = now("v")
        yield
        V(lambda e: e.tensor_mul(out=st_(8), in0=st_(7), in1=st_(4)))
        yield
        VD()
        V(lambda e: e.tensor_scalar_max(out=st_(8), in0=st_(8), scalar1=1e-5))
        yield
        VD()
        V(lambda e: e.reciprocal(out=st_(19), in_=st_(8)))
        yield
        VD()
        V(lambda e: e.tensor_scalar_mul(out=st_(9), in0=st_(19), scalar1=127.0))
        yield
        VD()
        V(lambda e: e.tensor_mul(out=st_(16), in0=st_(9), in1=wscal[:, 0:1]))
        yield
        V(lambda e: e.tensor_mul(out=st_(10), in0=st_(9), in1=st_(4)))
        yield
        V(lambda e: e.reciprocal(out=st_(11), in_=st_(16)))
        yield
        V(lambda e: e.tensor_mul(out=st_(17), in0=st_(9), in1=wscal[:, 1:2]))
        yield
        V(lambda e: e.tensor_scalar_mul(out=st_(11), in0=st_(11), scalar1=0.5))
        yield
        V(lambda e: e.reciprocal(out=st_(12), in_=st_(17)))
        yield

    pend = [None]

    def drain(n):
        g = pend[0]
        if g is None:
            return
        for _ in range(n):
            if next(g, StopIteration) is StopIteration:
                pend[0] = None
                return

    def drain_all():
        drain(1000)

    def rest_a(i):
        til = i % 2
        for c in range(8):
            T(lambda e, c=c: e.transpose(out=pT1[:, c * 128:(c + 1) * 128],
                                         in_=xn_tm[:, c * 128:(c + 1) * 128],
                                         identity=ident[:, :]),
              ([("v", xnb_v_d[i]), ("v", pt1_free_v[0])]
               + ([("ldc", ld0)] if i == 0 else [])) if c == 0 else (),
              inc=(c == 7))
        xnT_t[i] = now("t")
        V(lambda e, til=til: e.tensor_copy(
            out=xn_fm[:, :].rearrange("p (c t) -> p c t", c=8)[:, :, til * 128:(til + 1) * 128],
            in_=pT1[:, :].rearrange("p (c t) -> p c t", c=8)),
          [("t", xnT_t[i]), ("t", xnfm_rd_t[0])])
        pt1_free_v[0] = now("v")

    def rest_b(i):
        til = i % 2
        so = 32 * til
        xv = xbig[:, til * D:(til + 1) * D]
        st_ = lambda c: stat[:, so + c:so + c + 1]
        yb = outb[:, (1 - til) * D:(2 - til) * D]
        V(lambda e, xv=xv, yb=yb: e.tensor_scalar(
            out=yb, in0=xv, scalar1=st_(1), scalar2=st_(10),
            op0=ALU.subtract, op1=ALU.mult),
          [store_last[1 - til]] if store_last[1 - til] is not None else [])
        vy0 = now("v")
        Sc(lambda e, yb=yb: e.activation(out=yb, in_=yb,
                                         func=AF.Identity, bias=cst[:, 0:1],
                                         scale=1.0),
           [("v", vy0)])
        V(lambda e, yb=yb: e.tensor_scalar_add(out=xn_tm[:, :], in0=yb,
                                               scalar1=-MAGIC),
          [("s", now("s")), ("t", xnT_t[i])])
        q1_v = now("v")
        for c in range(8):
            T(lambda e, c=c: e.transpose(out=pT2[:, c * 128:(c + 1) * 128],
                                         in_=xn_tm[:, c * 128:(c + 1) * 128],
                                         identity=ident[:, :]),
              ([("v", q1_v), pt2_free[0]]) if c == 0 else (),
              inc=(c == 7))
        xqT_t[i] = now("t")
        Sc(lambda e, til=til: e.copy(
            out=xq_fmbuf[:, til * D:(til + 1) * D], in_=pT2[:, :]),
           [("t", xqT_t[i]), ("t", ffn_gu_t.get(i - 2, 0))])
        xq_evac_s[i] = now("s")
        pt2_free[0] = ("s", xq_evac_s[i])

    def stretch_proj(mid=None):
        xnfm_v = pt1_free_v[0]
        for cp in range(8):
            rot = cp % 4
            pj = (pA, pB, pS1, pS2)[rot]
            tw = [("v", xnfm_v)]
            if rot == 0:
                pf = pA_free
                tw += [pf[0]]
            elif rot == 1:
                pf = pB_free
                tw += [pf[0]]
            elif rot == 2:
                pf = None
                tw += [("v", ps1_free_v[0]), ("s", ps1_owner_s[0])]
            else:
                pf = None
                tw += [("v", ps2_free_v[0])]
            for kc in range(8):
                T(lambda e, cp=cp, kc=kc, pj=pj: e.matmul(
                    out=pj[:, 0:STR],
                    lhsT=wq[:, (kc * 8 + cp) * 128:(kc * 8 + cp + 1) * 128],
                    rhs=xn_fm[:, kc * STR:(kc + 1) * STR],
                    start=(kc == 0), stop=(kc == 7)), tw if kc == 0 else (),
                  inc=(kc == 7))
            par = cp % 2
            qo = par * STR
            ew = [("t", now("t"))]
            if qtmp_last[par] is not None:
                ew += list(qtmp_last[par])
            V(lambda e, cp=cp, pj=pj, qo=qo: e.tensor_scalar(
                out=qtmp[:, qo:qo + STR], in0=pj[:, 0:STR],
                scalar1=biases[:, cp:cp + 1],
                scalar2=None, op0=ALU.add), ew)
            vv = now("v")
            if pf is not None:
                pf[0] = ("v", vv)
            elif rot == 2:
                ps1_free_v[0] = vv
            else:
                ps2_free_v[0] = vv
            G(lambda e, cp=cp, qo=qo: e.tensor_copy(
                out=q_blk[0:64, :].rearrange("p (grp c) -> p grp c", c=128)[:, :, (2 * cp) * 8:(2 * cp + 1) * 8],
                in_=qtmp[0:64, qo:qo + STR].rearrange("p (grp t) -> p grp t", t=8)),
              [("v", vv)])
            Sc(lambda e, cp=cp, qo=qo: e.copy(
                out=q_blk[0:64, :].rearrange("p (grp c) -> p grp c", c=128)[:, :, (2 * cp + 1) * 8:(2 * cp + 2) * 8],
                in_=qtmp[64:128, qo:qo + STR].rearrange("p (grp t) -> p grp t", t=8)),
               [("v", vv)])
            qtmp_last[par] = (("g", now("g")), ("s", now("s")))
            if cp == 1 and mid is not None:
                mid()
        for lc in range(2):
            pj = pA if lc % 2 == 0 else pB
            pf = pA_free if lc % 2 == 0 else pB_free
            tw = [pf[0]]
            for kc in range(8):
                T(lambda e, lc=lc, kc=kc, pj=pj: e.matmul(
                    out=pj[:, 0:STR],
                    lhsT=wdn[:, (kc * 2 + lc) * 128:(kc * 2 + lc + 1) * 128],
                    rhs=xn_fm[:, kc * STR:(kc + 1) * STR],
                    start=(kc == 0), stop=(kc == 7)), tw if kc == 0 else (),
                  inc=(kc == 7))
            V(lambda e, lc=lc, pj=pj: e.tensor_scalar(
                out=lat_fm[:, lc * STR:(lc + 1) * STR], in0=pj[:, 0:STR],
                scalar1=biases[:, 8 + lc:9 + lc], scalar2=None, op0=ALU.add),
              [("t", now("t")), ("t", latfm_rd_t[0])] if lc == 0 else
              [("t", now("t"))])
            pf[0] = ("v", now("v"))
        xnfm_rd_t[0] = now("t")
        lat_v = now("v")
        npair = [0]
        for wmat, bc, isk in ((wk, 10, True), (wv, 18, False)):
            for cp in range(8):
                rot = npair[0] % 4
                npair[0] += 1
                pj = (pA, pB, pS1, pS2)[rot]
                tw = [("v", lat_v)]
                if rot == 0:
                    pf = pA_free
                    tw += [pf[0]]
                elif rot == 1:
                    pf = pB_free
                    tw += [pf[0]]
                elif rot == 2:
                    pf = None
                    tw += [("v", ps1_free_v[0]), ("s", ps1_owner_s[0])]
                else:
                    pf = None
                    tw += [("v", ps2_free_v[0])]
                for kc in range(2):
                    T(lambda e, cp=cp, kc=kc, pj=pj, wmat=wmat: e.matmul(
                        out=pj[:, 0:STR],
                        lhsT=wmat[:, (kc * 8 + cp) * 128:(kc * 8 + cp + 1) * 128],
                        rhs=lat_fm[:, kc * STR:(kc + 1) * STR],
                        start=(kc == 0), stop=(kc == 1)), tw if kc == 0 else (),
                      inc=(kc == 1))
                par = cp % 2
                qo = par * STR
                ew = [("t", now("t"))]
                if qtmp_last[par] is not None:
                    ew += list(qtmp_last[par])
                V(lambda e, cp=cp, pj=pj, bc=bc, qo=qo: e.tensor_scalar(
                    out=qtmp[:, qo:qo + STR], in0=pj[:, 0:STR],
                    scalar1=biases[:, bc + cp:bc + cp + 1],
                    scalar2=None, op0=ALU.add), ew)
                vv = now("v")
                if pf is not None:
                    pf[0] = ("v", vv)
                elif rot == 2:
                    ps1_free_v[0] = vv
                else:
                    ps2_free_v[0] = vv
                dst = k_blk if isk else v_bT
                VD()
                V(lambda e, cp=cp, qo=qo, dst=dst: e.tensor_copy(
                    out=dst[0:64, :].rearrange("p (grp c) -> p grp c", c=128)[:, :, (2 * cp) * 8:(2 * cp + 1) * 8],
                    in_=qtmp[0:64, qo:qo + STR].rearrange("p (grp t) -> p grp t", t=8)))
                Sc(lambda e, cp=cp, qo=qo, dst=dst: e.copy(
                    out=dst[0:64, :].rearrange("p (grp c) -> p grp c", c=128)[:, :, (2 * cp + 1) * 8:(2 * cp + 2) * 8],
                    in_=qtmp[64:128, qo:qo + STR].rearrange("p (grp t) -> p grp t", t=8)), [("v", vv)])
                qtmp_last[par] = (("v", now("v")), ("s", now("s")))
        latfm_rd_t[0] = now("t")
        return now("g"), now("s"), now("v")

    vT_t = {}
    vbs_v = {}
    sc_t = {}
    exp_s = {}
    av_t = {}
    norm_v = {}
    norm_g = {}
    avT_t = {}
    avfm_v = {}

    def head_vt(si, b, gmarsh, smarsh, vmarsh):
        g0 = b * 4
        tw = [("v", pt1_free_v[0])]
        if b == 0:
            tw += [("g", gmarsh), ("s", smarsh), ("v", vmarsh)]
        for gg in range(4):
            g = g0 + gg
            T(lambda e, g=g, gg=gg: e.transpose(
                out=pT1[:, gg * 64:(gg + 1) * 64],
                in_=v_bT[0:64, g * 128:(g + 1) * 128],
                identity=ident[0:64, 0:64]), tw if gg == 0 else (),
              inc=(gg == 3))
        vT_t[b] = now("t")

    def head_vbs(b, extra_t):
        o = (b % 2) * 260
        V(lambda e, o=o: e.tensor_copy(
            out=v_bs[:, o:o + 260].rearrange("p (g c) -> p g c", g=4)[:, :, 0:64],
            in_=pT1[:, 0:256].rearrange("p (g c) -> p g c", g=4)),
          [("t", max(vT_t[b], extra_t))])
        vbs_v[b] = now("v")
        pt1_free_v[0] = vbs_v[b]

    def head_scores(si, b):
        g0 = b * 4
        sw = [("v", vbs_v[b]), ("v", ps1_free_v[0])]
        if b >= 1:
            sw += [("s", exp_s[b - 1][1])]
        if si == 0 and b == 0:
            sw += [("ldc", ld0)]
        for gg in range(4):
            g = g0 + gg
            T(lambda e, g=g, gg=gg: e.matmul(
                out=pS1[:, gg * 128:(gg + 1) * 128],
                lhsT=k_blk[0:73, g * 128:(g + 1) * 128],
                rhs=q_blk[0:73, g * 128:(g + 1) * 128],
                start=True, stop=True), sw if gg == 0 else (),
              inc=(gg == 3))
        sc_t[b] = now("t")
        exp_h = []
        for h in range(2):
            Sc(lambda e, h=h: e.activation(
                out=probs[:, h * 256:(h + 1) * 256],
                in_=pS1[:, h * 256:(h + 1) * 256], func=AF.Exp,
                bias=0.0, scale=cst[:, 1:2]),
               [("t", sc_t[b])] if h == 0 else ())
            exp_h.append(now("s"))
        exp_s[b] = (exp_h[0], exp_h[1])
        ps1_owner_s[0] = exp_h[1]

    def attn_head(si, b, gmarsh, smarsh, vmarsh):
        head_vt(si, b, gmarsh, smarsh, vmarsh)
        head_vbs(b, 0)
        head_scores(si, b)

    def attn_step(si, b, gmarsh, smarsh, vmarsh, with_head):
        g0 = b * 4
        if with_head:
            head_vt(si, b + 1, gmarsh, smarsh, vmarsh)
        for gg in range(4):
            aw = ()
            if gg == 0:
                aw = [("s", exp_s[b][0]), ("v", pav_free_v[0])]
            elif gg == 2:
                aw = [("s", exp_s[b][1])]
            T(lambda e, gg=gg, o=(b % 2) * 260: e.matmul(
                out=pAV[:, gg * 65:(gg + 1) * 65],
                lhsT=probs[:, gg * 128:(gg + 1) * 128],
                rhs=v_bs[:, o + gg * 65:o + (gg + 1) * 65],
                start=True, stop=True), aw, inc=(gg == 3))
        av_t[b] = now("t")
        if with_head:
            head_vbs(b + 1, av_t.get(b - 1, 0))
            head_scores(si, b + 1)
        V(lambda e: e.reciprocal(out=stat[:, 28:32], in_=pAV[:, 64:260:65]),
          [("t", av_t[b])])
        VD()
        nv = []
        for gg in range(4):
            V(lambda e, gg=gg: e.tensor_scalar(
                out=av_sb[:, gg * 64:(gg + 1) * 64],
                in0=pAV[:, gg * 65:gg * 65 + 64],
                scalar1=stat[:, 28 + gg:29 + gg], scalar2=None, op0=ALU.mult))
            nv.append(now("v"))
        norm_v[b] = now("v")
        pav_free_v[0] = norm_v[b]
        for gg in range(4):
            pw = [("v", nv[gg])]
            if gg == 0:
                pw += [avfm_v[b - 1] if b >= 1 else pt2_free[0]]
            T(lambda e, gg=gg: e.transpose(
                out=pT2[0:64, gg * 128:(gg + 1) * 128],
                in_=av_sb[:, gg * 64:(gg + 1) * 64],
                identity=ident[:, :]),
              pw, inc=(gg == 3))
        avT_t[b] = now("t")
        ew = [("t", avT_t[b])]
        if b % 4 == 0:
            ew += [("t", wout_rd_t[0])]
        gl0 = g0 % 16
        for p in range(2):
            V(lambda e, p=p, gl0=gl0: e.tensor_copy(
                out=av_fm[p * 64:(p + 1) * 64, :]
                    .rearrange("q (c gl2 t) -> q gl2 c t", c=8, gl2=16)[:, gl0:gl0 + 4, :, :],
                in_=pT2[0:64, 0:512]
                    .rearrange("q (gg c2 pp t) -> q gg c2 pp t", gg=4, c2=8, pp=2, t=8)[:, :, :, p, :]),
              ew if p == 0 else ())
        avfm_v[b] = ("v", now("v"))
        pt2_free[0] = avfm_v[b]
        drain(3)

    vs2_d = {}

    def ffn_a(i, avfm_ready):
        til = i % 2
        so = 32 * til
        st_ = lambda c: stat[:, so + c:so + c + 1]
        xv = xbig[:, til * D:(til + 1) * D]
        ss = (1 - til) * D
        stg = outb[:, ss:ss + D]
        for half in range(2):
            pj = pA if half == 0 else pB
            pf = pA_free if half == 0 else pB_free
            tw = [avfm_ready, pf[0]]
            T(lambda e, half=half, pj=pj: e.matmul(
                out=pj[:, :], lhsT=onesrow[:, :],
                rhs=boutb[:, half * 512:(half + 1) * 512],
                start=True, stop=False), tw, inc=False)
            for c in range(8):
                T(lambda e, c=c, half=half, pj=pj: e.matmul(
                    out=pj[:, :], lhsT=av_fm[:, c * 128:(c + 1) * 128],
                    rhs=wo[:, (c * 8 + half * 4) * 128:(c * 8 + half * 4 + 4) * 128],
                    start=False, stop=(c == 7)), inc=(c == 7))
        wout_t = now("t")
        wout_rd_t[0] = wout_t
        stw = [("t", wout_t)]
        if i - 1 >= 0:
            stw += [(st_of(i - 1), store_cnt[i - 1])]
        V(lambda e, stg=stg: e.scalar_tensor_tensor(
            out=stg[:, 0:512], in0=pA[:, :], scalar=1.0, in1=xv[:, 0:512],
            op0=ALU.mult, op1=ALU.add), stw)
        stage1_v = now("v")
        pA_free[0] = ("v", stage1_v)
        V(lambda e, stg=stg: e.scalar_tensor_tensor(
            out=stg[:, 512:1024], in0=pB[:, :], scalar=1.0, in1=xv[:, 512:1024],
            op0=ALU.mult, op1=ALU.add))
        stage2_v = now("v")
        pB_free[0] = ("v", stage2_v)
        xlast_v[i] = stage2_v
        xlast_s[i] = sq_s[i]
        g2_v = {}
        stt2_v = {}
        gate_t = {}
        up_t = {}
        tanh_s = {}
        for j in range(8):
            gw = [("v", g2_v[j - 1] if j >= 1 else stage1_v)]
            if j == 0:
                gw += [("s", xq_evac_s[i])]
            for c in range(8):
                T(lambda e, c=c, j=j, til=til: e.matmul(
                    out=pA[:, :], lhsT=xq_fmbuf[:, til * D + c * 128:til * D + (c + 1) * 128],
                    rhs=wg[:, c * INNER + j * 512:c * INNER + (j + 1) * 512],
                    start=(c == 0), stop=(c == 7)), gw if c == 0 else (),
                  inc=(c == 7))
            gate_t[j] = now("t")
            pu = pB if j % 2 == 0 else pC
            puf = pB_free if j % 2 == 0 else pC_free
            uw = [puf[0]]
            for c in range(8):
                T(lambda e, c=c, j=j, til=til, pu=pu: e.matmul(
                    out=pu[:, :], lhsT=xq_fmbuf[:, til * D + c * 128:til * D + (c + 1) * 128],
                    rhs=wu[:, c * INNER + j * 512:c * INNER + (j + 1) * 512],
                    start=(c == 0), stop=(c == 7)), uw if c == 0 else (),
                  inc=(c == 7))
            up_t[j] = now("t")
            V(lambda e: e.tensor_scalar(out=g2st[:, :], in0=pA[:, :],
                                        scalar1=st_(11), scalar2=None,
                                        op0=ALU.mult), [("t", gate_t[j])])
            g2_v[j] = now("v")
            Sc(lambda e: e.activation(out=tst[:, 0:512], in_=g2st[:, :],
                                      func=AF.Tanh,
                                      bias=0.0, scale=1.0), [("v", g2_v[j])])
            tanh_s[j] = now("s")
            if j >= 1:
                V(lambda e, j=j: e.tensor_reduce(
                    out=stat[:, 33:34], in_=big[:, (j - 1) * 512:j * 512],
                    axis=AX, op=ALU.max, apply_absolute_value=True))
            else:
                VD()
            V(lambda e: e.scalar_tensor_tensor(
                out=g2st[:, :], in0=tst[:, 0:512], scalar=1.0, in1=g2st[:, :],
                op0=ALU.add, op1=ALU.mult), [("s", tanh_s[j])])
            if j >= 2:
                V(lambda e: e.tensor_max(out=stat[:, 34:35], in0=stat[:, 34:35],
                                         in1=stat[:, 33:34]))
            elif j == 1:
                V(lambda e: e.tensor_copy(out=stat[:, 34:35], in_=stat[:, 33:34]))
            else:
                VD()
            V(lambda e, j=j, pu=pu: e.scalar_tensor_tensor(
                out=big[:, j * 512:(j + 1) * 512], in0=pu[:, :], scalar=st_(12),
                in1=g2st[:, :], op0=ALU.mult, op1=ALU.mult),
              [("t", up_t[j]), ("s", last_round_s[0])])
            stt2_v[j] = now("v")
            puf[0] = ("v", stt2_v[j])
            drain(2)
        ffn_gu_t[i] = up_t[7]
        pA_free[0] = ("v", g2_v[7])
        VD()
        V(lambda e: e.tensor_reduce(
            out=stat[:, 33:34], in_=big[:, 7 * 512:8 * 512],
            axis=AX, op=ALU.max, apply_absolute_value=True))
        VD()
        V(lambda e: e.tensor_max(out=stat[:, 34:35], in0=stat[:, 34:35],
                                 in1=stat[:, 33:34]))
        VD()
        V(lambda e: e.tensor_scalar_max(out=stat[:, 35:36], in0=stat[:, 34:35],
                                        scalar1=1e-5))
        VD()
        V(lambda e: e.reciprocal(out=stat[:, 36:37], in_=stat[:, 35:36]))
        VD()
        V(lambda e: e.tensor_scalar_mul(out=st_(13), in0=stat[:, 36:37],
                                        scalar1=127.0))
        VD()
        V(lambda e: e.tensor_mul(out=stat[:, 37:38], in0=st_(13),
                                 in1=wscal[:, 2:3]))
        VD()
        V(lambda e: e.reciprocal(out=st_(14), in_=stat[:, 37:38]))
        vs2_d[i] = now("v")

    def ffn_b(i, mid=None):
        til = i % 2
        so = 32 * til
        st_ = lambda c: stat[:, so + c:so + c + 1]
        os_ = til * D
        ss = (1 - til) * D
        stg = outb[:, ss:ss + D]
        vs2 = vs2_d[i]
        sub_v = {}
        xq2f_v = {}
        trans_t = {}
        round_s = {}
        down_t = {}
        for j in range(9):
            if j < 8:
                par = j % 2
                Sc(lambda e, j=j: e.activation(
                    out=g2st[:, :], in_=big[:, j * 512:(j + 1) * 512],
                    func=AF.Identity, bias=cst[:, 0:1], scale=st_(13)),
                   [("v", vs2 if j == 0 else sub_v[j - 1])])
                round_s[j] = now("s")
                V(lambda e, par=par: e.tensor_scalar_add(
                    out=tst[:, par * 512:(par + 1) * 512], in0=g2st[:, :],
                    scalar1=-MAGIC),
                  [("s", round_s[j]), ("t", trans_t.get(j - 2, 0))])
                sub_v[j] = now("v")
                tw = [("v", sub_v[j])]
                if j >= 2:
                    tw += [("v", xq2f_v[j - 2])]
                else:
                    tw += [("v", pt1_free_v[0])]
                for c in range(4):
                    T(lambda e, c=c, par=par: e.transpose(
                        out=pT1[:, par * 512 + c * 128:par * 512 + (c + 1) * 128],
                        in_=tst[:, par * 512 + c * 128:par * 512 + (c + 1) * 128],
                        identity=ident[:, :]), tw if c == 0 else (),
                      inc=(c == 3))
                trans_t[j] = now("t")
                V(lambda e, par=par: e.tensor_copy(
                    out=xq2f[:, par * 512:(par + 1) * 512],
                    in_=pT1[:, par * 512:(par + 1) * 512]),
                  [("t", trans_t[j]), ("t", down_t.get(j - 2, 0))])
                xq2f_v[j] = now("v")
                pt1_free_v[0] = xq2f_v[j]
                if j >= 2:
                    drain(3)
            if j >= 1:
                jj = j - 1
                dw = [("v", xq2f_v[jj])]
                if jj == 0:
                    dw += [("s", ps1_owner_s[0]), ("v", ps2_free_v[0])]
                for c in range(4):
                    ic = jj * 4 + c
                    for half in range(2):
                        pj = pS1 if half == 0 else pS2
                        T(lambda e, c=c, jj=jj, ic=ic, half=half, pj=pj: e.matmul(
                            out=pj[:, :],
                            lhsT=xq2f[:, (jj % 2) * 512 + c * 128:(jj % 2) * 512 + (c + 1) * 128],
                            rhs=wd[:, ic * 1024 + half * 512:ic * 1024 + (half + 1) * 512],
                            start=(ic == 0), stop=(ic == 31)),
                          dw if (c == 0 and half == 0) else (),
                          inc=(c == 3 and half == 1))
                down_t[jj] = now("t")
                if jj == 0 and mid is not None:
                    mid()
        last_round_s[0] = round_s[7]
        tdown = now("t")
        V(lambda e, os_=os_, stg=stg: e.scalar_tensor_tensor(
            out=outb[:, os_:os_ + 512], in0=pS1[:, :], scalar=st_(14),
            in1=stg[:, 0:512], op0=ALU.mult, op1=ALU.add), [("t", tdown)])
        ps1_free_v[0] = now("v")
        V(lambda e, os_=os_, stg=stg: e.scalar_tensor_tensor(
            out=outb[:, os_ + 512:os_ + 1024], in0=pS2[:, :], scalar=st_(14),
            in1=stg[:, 512:1024], op0=ALU.mult, op1=ALU.add))
        ps2_free_v[0] = now("v")
        dma(y_d[i * TILE:(i + 1) * TILE, :], outb[:, os_:os_ + D],
            [("v", now("v"))], sem=st_of(i))
        store_cnt[i] = now(st_of(i))
        store_last[til] = (st_of(i), store_cnt[i])

    n_tiles = n_tok // TILE
    carry_rest_b = [None]
    for si in range(n_str):
        i0, i1 = 2 * si, 2 * si + 1
        if si == 0:
            load_x(i0)
            load_x(i1)
            pend[0] = front_stats_gen(i0)
            drain_all()
            rest_a(i0)
            rest_b(i0)
            pend[0] = front_stats_gen(i1)
            drain_all()
            rest_a(i1)
            rest_b(i1)
        gmarsh, smarsh, vmarsh = stretch_proj(mid=carry_rest_b[0])
        carry_rest_b[0] = None
        attn_head(si, 0, gmarsh, smarsh, vmarsh)
        for b in range(4):
            attn_step(si, b, gmarsh, smarsh, vmarsh, with_head=True)
        ffn_a(i0, avfm_v[3])
        if i0 + 2 < n_tiles:
            load_x(i0 + 2)
            pend[0] = front_stats_gen(i0 + 2)
        attn_step(si, 4, gmarsh, smarsh, vmarsh, with_head=True)
        attn_step(si, 5, gmarsh, smarsh, vmarsh, with_head=True)
        ffn_b(i0)
        attn_step(si, 6, gmarsh, smarsh, vmarsh, with_head=True)
        attn_step(si, 7, gmarsh, smarsh, vmarsh, with_head=False)
        ffn_a(i1, avfm_v[7])
        if i1 + 2 < n_tiles:
            load_x(i1 + 2)
            drain_all()
            rest_a(i0 + 2)

            def mid_cb(i2=i0 + 2, i3=i1 + 2):
                rest_b(i2)
                pend[0] = front_stats_gen(i3)
            ffn_b(i1, mid=mid_cb)
            drain_all()
            rest_a(i1 + 2)
            carry_rest_b[0] = (lambda i3=i1 + 2: rest_b(i3))
        else:
            ffn_b(i1)
    emit("sync", lambda e: e.nop(),
         [("st0", now("st0")), ("st1", now("st1"))], ())

    if max_events is not None:
        del events[max_events:]
    if dbg:
        dbg_list = [('xn_tm', xn_tm), ('stat', stat),
                    ('xn_fm', xn_fm), ('qtmp', qtmp), ('q_blk', q_blk),
                    ('k_blk', k_blk), ('v_bT', v_bT), ('lat_fm', lat_fm),
                    ('probs', probs), ('av_sb', av_sb), ('av_fm', av_fm),
                    ('big', big), ('tst', tst), ('g2st', g2st),
                    ('outb', outb), ('v_bs', v_bs), ('xq_fmbuf', xq_fmbuf),
                    ('xq2f', xq2f), ('wscal', wscal), ('xbig', xbig),
                    ('biases_s', biases)]
        import os as _os
        _sel = _os.environ.get('KDBG', '')
        if _sel:
            dbg_list = [(n_, t_) for n_, t_ in dbg_list if n_ in _sel.split(',')]
        pre = []
        _noinit = {'q_blk', 'k_blk', 'v_bs', 'big', 'xbig', 'biases_s'}
        for nm, tt_ in dbg_list:
            if nm in _noinit:
                continue
            pre.append(("gpsimd", (lambda s_: (lambda e: e.memset(s_[:, :], 0.0)))(tt_),
                        (), (("g", 0),)))
        events[0:0] = pre
        allw = [(k_, now(k_)) for k_ in
                ('t', 'v', 's', 'g', 'ldc', 'l0', 'l1', 'l2', 'l3',
                 'ldx0', 'ldx1', 'st0', 'st1')]
        _rows = {'q_blk': 73, 'k_blk': 73}
        for nm, tt_ in dbg_list:
            ap_ = tt_.ap()
            r_ = _rows.get(nm, ap_.shape[0])
            dd = nc.dram_tensor('dbg_' + nm, [r_, ap_.shape[1]], ap_.dtype,
                                kind='ExternalOutput').ap()
            emit('sync', lambda e, d=dd, s_=tt_, r_=r_: e.dma_start(out=d, in_=s_[0:r_, :]),
                 allw, [('st0', 16)])
        emit('sync', lambda e: e.nop(), [('st0', now('st0'))], ())
    sem_names = ["ldc", "l0", "l1", "l2", "l3", "ldx0", "ldx1",
                 "st0", "st1", "t", "v", "s", "g"]
    import contextlib
    with contextlib.ExitStack() as stack:
        sems = {n_: stack.enter_context(nc.semaphore(n_)) for n_ in sem_names}
        block = stack.enter_context(nc.Block())

        def runner(name):
            def run(eng):
                for e_name, fn, waits, incs in events:
                    if e_name != name:
                        continue
                    for sname, val in waits:
                        if val > 0:
                            eng.wait_ge(sems[sname], val)
                    ins = fn(eng)
                    for sname, amt in incs:
                        ins.then_inc(sems[sname], amt)
            return run

        block.sync(runner("sync"))
        block.tensor(runner("tensor"))
        block.vector(runner("vector"))
        block.scalar(runner("scalar"))
        block.gpsimd(runner("gpsimd"))
    return nc


def _prep_weights(inp):
    w = {}
    wq = np.asarray(inp["L0_attn_Wq"], np.float32)
    w["wq"] = np.ascontiguousarray(
        wq.reshape(8, 128, 8, 128).transpose(1, 0, 2, 3).reshape(128, 64 * 128))
    wdn = np.asarray(inp["L0_attn_Wdown"], np.float32)
    w["wdn"] = np.ascontiguousarray(
        wdn.reshape(8, 128, 2, 128).transpose(1, 0, 2, 3).reshape(128, 16 * 128))
    wup = np.asarray(inp["L0_attn_Wup"], np.float32).reshape(256, 16, 2, 64)
    wkm = np.ascontiguousarray(wup[:, :, 0].reshape(256, 1024))
    wvm = np.ascontiguousarray(wup[:, :, 1].reshape(256, 1024))
    w["wk"] = np.ascontiguousarray(
        wkm.reshape(2, 128, 8, 128).transpose(1, 0, 2, 3).reshape(128, 16 * 128))
    w["wv"] = np.ascontiguousarray(
        wvm.reshape(2, 128, 8, 128).transpose(1, 0, 2, 3).reshape(128, 16 * 128))
    wo = np.asarray(inp["L0_attn_Wout"], np.float32)
    w["wo"] = np.ascontiguousarray(
        wo.reshape(8, 128, 8, 128).transpose(1, 0, 2, 3).reshape(128, 64 * 128))
    wgT = np.ascontiguousarray(np.asarray(inp["L0_ffn_w_gate"], np.float32).T)
    wuT = np.ascontiguousarray(np.asarray(inp["L0_ffn_w_up"], np.float32).T)
    wdT = np.ascontiguousarray(np.asarray(inp["L0_ffn_w_down"], np.float32).T)
    w["wg"] = np.ascontiguousarray(
        wgT.reshape(8, 128, 4096).transpose(1, 0, 2).reshape(128, 8 * 4096))
    w["wu"] = np.ascontiguousarray(
        wuT.reshape(8, 128, 4096).transpose(1, 0, 2).reshape(128, 8 * 4096))
    w["wd"] = np.ascontiguousarray(
        wdT.reshape(32, 128, 1024).transpose(1, 0, 2).reshape(128, 32 * 1024))
    bias = np.zeros((128, 26), np.float32)
    bias[:, 0:8] = np.asarray(inp["L0_attn_bq"], np.float32).reshape(8, 128).T
    bias[:, 8:10] = np.asarray(inp["L0_attn_bdown"], np.float32).reshape(2, 128).T
    bup = np.asarray(inp["L0_attn_bup"], np.float32).reshape(16, 2, 64)
    bias[:, 10:18] = np.ascontiguousarray(bup[:, 0].reshape(-1)).reshape(8, 128).T
    bias[:, 18:26] = np.ascontiguousarray(bup[:, 1].reshape(-1)).reshape(8, 128).T
    w["bias"] = bias
    w["bout"] = np.asarray(inp["L0_attn_bout"], np.float32).reshape(1, D)
    return w


def kernel(**inputs):
    x = np.asarray(inputs["L0_attn_x"], np.float32)
    B, T_, D_ = x.shape
    n_tok = (B * T_) // NCORE
    w = _prep_weights(inputs)
    nc = build_program(n_tok)
    xs = x.reshape(NCORE, n_tok, D_)
    in_maps = [dict(w, x=np.ascontiguousarray(xs[c])) for c in range(NCORE)]
    res = run_bass_kernel_spmd(nc, in_maps, core_ids=list(range(NCORE)))
    y = np.stack([np.asarray(r["y"]) for r in res.results])
    return np.ascontiguousarray(y.reshape(B, T_, D_))
